# version 06
# baseline (speedup 1.0000x reference)
from contextlib import ExitStack
import numpy as np
import ml_dtypes
import concourse.bass as bass
import concourse.mybir as mybir
from concourse.bass_utils import run_bass_kernel_spmd

def _mk_nc():
    return bass.Bass("TRN2", target_bir_lowering=False, detect_race_conditions=False)


F32 = mybir.dt.float32
BF16 = mybir.dt.bfloat16
AF = mybir.ActivationFunctionType
ALU = mybir.AluOpType
AX = mybir.AxisListType
bf16 = ml_dtypes.bfloat16

D = 2048
BATCH = 4
S = 2048
TC = 1024
NCORE = 8
DIN = 4096
DST = 128
NH = 64
DH = 64
CONV_DIM = DIN + 2 * DST
DPROJ = 2 * DIN + 2 * DST + NH
DPROJ_PAD = 8576
CHUNK = 256
NCH = 4
H0 = 5632
H0_PAD = 5632
NHEAD_ATT = 16
HDA = 128

KT = D // 128


def _retile_kt(wt):
    K, M = wt.shape
    assert K % 128 == 0 and M % 128 == 0
    return np.ascontiguousarray(
        wt.reshape(K // 128, 128, M // 128, 128).transpose(2, 0, 1, 3))


def _feat_tiles(xT):
    K, T = xT.shape
    assert K % 128 == 0
    return np.ascontiguousarray(xT.reshape(K // 128, 128, T))



def build_l1():
    nc = _mk_nc()
    x_in = nc.dram_tensor("x_in", [KT, 128, TC], F32, kind="ExternalInput").ap()
    w_in = nc.dram_tensor("w_in", [67, KT, 128, 128], F32, kind="ExternalInput").ap()
    wg_in = nc.dram_tensor("wg_in", [3, 6, 128, D], F32, kind="ExternalInput").ap()
    zx_spill = nc.dram_tensor("zx_spill", [NCH, 67, 128, CHUNK], F32, kind="ExternalOutput").ap()
    gpart = nc.dram_tensor("gpart", [1, 8], F32, kind="ExternalOutput").ap()
    gb = nc.dram_tensor("gb", [128, 4], F32).ap()

    with ExitStack() as st:
        sb = lambda name, shape, dt: st.enter_context(nc.sbuf_tensor(name, shape, dt))
        pt = lambda name, shape, dt=F32: st.enter_context(nc.psum_tensor(name, shape, dt))
        sem = lambda name: st.enter_context(nc.semaphore(name))
        xbuf = sb("xbuf", [128, KT * TC], BF16)
        wbuf = sb("wbuf", [128, 2 * KT * 128], BF16)
        stage = sb("stage", [128, 4 * 512], F32)
        gwbuf = sb("gwbuf", [128, 6 * D], F32)
        gcol = sb("gcol", [128, 4], F32)
        grow = sb("grow", [1, 128], F32)
        gout = sb("gout", [1, 8], F32)
        ps = [pt(f"ps{i}", [128, 512]) for i in range(8)]
        ldg = sem("ldg"); wld = sem("wld"); msem = sem("msem")
        wsl = [sem("wslA"), sem("wslB")]
        ssl = [sem(f"ssl{i}") for i in range(4)]
        esem = sem("esem"); ssem = sem("ssem"); gsem = sem("gsem")

        with nc.Block() as block:
            @block.sync
            def _(sync):
                for j in range(3):
                    if j:
                        sync.wait_ge(gsem, j)
                    sync.dma_start(out=gwbuf[:, :].rearrange("p (r n) -> p r n", r=6),
                                   in_=wg_in[j]).then_inc(wld, 16)

            @block.vector
            def _(vector):
                for j in range(3):
                    vector.wait_ge(wld, 16 * (j + 1))
                    vector.tensor_reduce(gcol[:, j:j + 1], gwbuf[:, :], axis=AX.X,
                                         op=ALU.add, apply_absolute_value=True).then_inc(gsem, 1)

            @block.gpsimd
            def _(gpsimd):
                gpsimd.memset(gcol[:, 3:4], 0.0)
                gpsimd.memset(gout[:, :], 0.0)

        with nc.Block() as block:
            @block.sync
            def _(sync):
                sync.dma_start(out=gb[:, :], in_=gcol[:, :]).then_inc(wld, 16)
                sync.wait_ge(wld, 64)
                for j in range(3):
                    sync.dma_start(
                        out=grow[:, :],
                        in_=bass.AP(gb.tensor, j, [[4, 1], [4, 128]]),
                        allow_slow_non_contiguous=True).then_inc(wld, 16)
                    sync.wait_ge(wld, 64 + 16 * (j + 1))
                    sync.wait_ge(gsem, 3 + j + 1)

            @block.vector
            def _(vector):
                for j in range(3):
                    vector.wait_ge(wld, 64 + 16 * (j + 1))
                    vector.tensor_reduce(gout[:, j:j + 1], grow[:, :], axis=AX.X,
                                         op=ALU.add).then_inc(gsem, 1)

        with nc.Block() as block:
            @block.sync
            def _(sync):
                sync.dma_start(out=gpart, in_=gout[:, :]).then_inc(ssem, 16)
                sync.wait_ge(ssem, 16)

        with nc.Block() as block:
            @block.gpsimd
            def _(gpsimd):
                gpsimd.dma_start(
                    out=xbuf[:, :].rearrange("p (kk t) -> p kk t", kk=KT),
                    in_=x_in.rearrange("kk p t -> p kk t")).then_inc(ldg, 16)
                for m in range(67):
                    if m >= 2:
                        gpsimd.wait_ge(msem, 2 * (m - 1))
                    sl = m % 2
                    gpsimd.dma_start(
                        out=wbuf[:, sl * KT * 128:(sl + 1) * KT * 128].rearrange(
                            "p (kk mc) -> p kk mc", kk=KT),
                        in_=w_in[m].rearrange("kk p mc -> p kk mc")).then_inc(wsl[sl], 16)

            @block.tensor
            def _(tensor):
                tensor.wait_ge(ldg, 16)
                for m in range(67):
                    tensor.wait_ge(wsl[m % 2], 16 * (m // 2 + 1))
                    sl = m % 2
                    for n in range(2):
                        u = 2 * m + n
                        b = u % 8
                        if u >= 8:
                            tensor.wait_ge(esem, u - 7)
                        for kk in range(KT):
                            ins = tensor.matmul(
                                ps[b][:, :],
                                wbuf[:, sl * KT * 128 + kk * 128: sl * KT * 128 + (kk + 1) * 128],
                                xbuf[:, kk * TC + n * 512: kk * TC + (n + 1) * 512],
                                start=(kk == 0), stop=(kk == KT - 1))
                        ins.then_inc(msem, 1)

            @block.scalar
            def _(scalar):
                for m in range(67):
                    for n in range(2):
                        u = 2 * m + n
                        b = u % 8
                        s = u % 4
                        if u >= 4:
                            scalar.wait_ge(ssl[s], 32 * (u // 4))
                        scalar.wait_ge(msem, u + 1)
                        scalar.copy(stage[:, s * 512:(s + 1) * 512], ps[b][:, :]).then_inc(esem, 1)
                        scalar.wait_ge(esem, u + 1)
                        c0 = 2 * n
                        scalar.dma_start(out=zx_spill[c0, m],
                                         in_=stage[:, s * 512:s * 512 + 256]).then_inc(ssl[s], 16)
                        scalar.dma_start(out=zx_spill[c0 + 1, m],
                                         in_=stage[:, s * 512 + 256:(s + 1) * 512]).then_inc(ssl[s], 16)
                for s in range(4):
                    scalar.wait_ge(ssl[s], 32 * (34 if s < 2 else 33))
    return nc



def _prep_l1(inputs):
    u = np.asarray(inputs["L0_attn_u"])
    W_in = np.asarray(inputs["L0_attn_W_in"])
    w1 = np.asarray(inputs["L0_ffn_w1"])
    w2 = np.asarray(inputs["L0_ffn_w2"])
    w3 = np.asarray(inputs["L0_ffn_w3"])

    WT = np.zeros((D, DPROJ_PAD), dtype=np.float32)
    WT[:, :DPROJ] = W_in.T
    w_in_t = _retile_kt(WT)

    in_maps = []
    for c in range(NCORE):
        p, h = c // 2, c % 2
        uT = np.ascontiguousarray(u[p].T[:, h * TC:(h + 1) * TC])
        x_tiles = _feat_tiles(uT)
        gsl = np.zeros((3, 768, D), dtype=np.float32)
        gsl[0, :704] = w1[c * 704:(c + 1) * 704]
        gsl[1, :704] = w2[c * 704:(c + 1) * 704]
        gsl[2, :704] = w3.T[c * 704:(c + 1) * 704]
        gslt = gsl.reshape(3, 6, 128, D)
        in_maps.append({"x_in": x_tiles, "w_in": w_in_t, "wg_in": gslt})
    return in_maps


def kernel(**inputs):
    in_maps = _prep_l1(inputs)
    nc = build_l1()
    res = run_bass_kernel_spmd(nc, in_maps, core_ids=list(range(NCORE)))
    raise NotImplementedError


class CT:
    def __init__(self):
        self.n = {}

    def inc(self, name, k=1):
        self.n[name] = self.n.get(name, 0) + k
        return self.n[name]

    def val(self, name):
        return self.n.get(name, 0)



def build_l2a(stages=('conv','dt','tr','cb','lm','st')):
    nc = _mk_nc()
    zx = nc.dram_tensor("zx", [NCH, 67, 128, CHUNK], F32, kind="ExternalInput").ap()
    halo_in = nc.dram_tensor("halo_in", [34, 128, 4], F32, kind="ExternalInput").ap()
    convw = nc.dram_tensor("convw", [34, 128, 4], F32, kind="ExternalInput").ap()
    convb = nc.dram_tensor("convb", [128, 34], F32, kind="ExternalInput").ap()
    dtbias = nc.dram_tensor("dtbias", [64, 1], F32, kind="ExternalInput").ap()
    alog = nc.dram_tensor("alog", [64, 1], F32, kind="ExternalInput").ap()
    maskadd = nc.dram_tensor("maskadd", [2, 128, 2048], F32, kind="ExternalInput").ap()
    id64f = nc.dram_tensor("id64f", [64, 64], F32, kind="ExternalInput").ap()
    idbf = nc.dram_tensor("idbf", [128, 128], BF16, kind="ExternalInput").ap()

    m_spill = nc.dram_tensor("m_spill", [NCH, 2, 8, 128, 2048], BF16, kind="ExternalOutput").ap()
    xcT_spill = nc.dram_tensor("xcT_spill", [NCH, 2, 128, 4096], BF16, kind="ExternalOutput").ap()
    st_spill = nc.dram_tensor("st_spill", [NCH, 128, 4096], BF16, kind="ExternalOutput").ap()
    dacs_out = nc.dram_tensor("dacs_out", [NCH, 64, 256], F32, kind="ExternalOutput").ap()
    cfeat_out = nc.dram_tensor("cfeat_out", [NCH, 128, 256], BF16, kind="ExternalOutput").ap()
    carry_out = nc.dram_tensor("carry_out", [128, 4096], F32, kind="ExternalOutput").ap()

    dab = nc.dram_tensor("dab", [NCH, 64, 256], F32).ap()

    c_ = CT()

    with ExitStack() as stk:
        sb = lambda name, shape, dt: stk.enter_context(nc.sbuf_tensor(name, shape, dt))
        sem = lambda name: stk.enter_context(nc.semaphore(name))
        conv_in = sb("conv_in", [128, 34 * 260], F32)
        convw_sb = sb("convw_sb", [128, 34 * 4], F32)
        convb_sb = sb("convb_sb", [128, 34], F32)
        cacc = sb("cacc", [128, 2 * 256], F32)
        xconv = sb("xconv", [128, 34 * 256], BF16)
        dts = sb("dts", [64, 10 * 256], F32)
        dtbuf = sb("dtbuf", [64, 256], F32)
        dacs = sb("dacs", [64, 256], F32)
        cdcol = sb("cdcol", [64, 1], F32)
        onesrow64 = sb("onesrow64", [64, 256], F32)
        onesf = sb("onesf", [1, 128], F32)
        id64 = sb("id64", [64, 64], F32)
        idb = sb("idb", [128, 128], BF16)
        cmbT = sb("cmbT", [128, 2 * 64], F32)
        wT = sb("wT", [128, 2 * 64], F32)
        bT = sb("bT", [128, 2 * 128], BF16)
        bwT = sb("bwT", [128, 8 * 128], BF16)
        cd_u = sb("cd_u", [128, 64], F32)
        cdrow = sb("cdrow", [1, 64], F32)
        dabc = sb("dabc", [128, 2048], F32)
        argb = sb("argb", [128, 2048], F32)
        lm = sb("lm", [128, 2048], BF16)
        cbm = sb("cbm", [128, 2 * 2048], BF16)
        mbuf = sb("mbuf", [128, 2048], BF16)
        maska = sb("maska", [128, 2 * 2048], F32)
        xcT = sb("xcT", [128, 2 * 4096], BF16)
        states = sb("states", [128, 4096], BF16)
        carry = sb("carry", [128, 4096], F32)
        big0 = stk.enter_context(nc.psum_tensor("big0", [128, 2048], F32))
        big1 = stk.enter_context(nc.psum_tensor("big1", [128, 1024], F32))
        pbf = stk.enter_context(nc.psum_tensor("pbf", [128, 2048], BF16))
        ld = sem("ld")
        bc = sem("bc")
        dw = sem("dw")
        msp = sem("msp")
        so = sem("so")
        ts = sem("ts")
        vs = sem("vs")
        as_ = sem("as_")

        with nc.Block() as block:
            @block.sync
            def _(sync):
                sync.dma_start(out=convw_sb[:, :].rearrange("p (t w) -> p t w", t=34),
                               in_=convw.rearrange("t p w -> p t w")).then_inc(ld, 16)
                sync.dma_start(out=convb_sb[:, :], in_=convb).then_inc(ld, 16)
                sync.dma_start(out=dts[:, 2560 - 256:2560 - 255], in_=dtbias).then_inc(ld, 16)
                sync.dma_start(out=dts[:, 2560 - 255:2560 - 254], in_=alog).then_inc(ld, 16)
                sync.dma_start(out=id64[:, :], in_=id64f).then_inc(ld, 16)
                sync.dma_start(out=idb[:, :], in_=idbf).then_inc(ld, 16)
                sync.dma_start(out=maska[:, :].rearrange("p (j n) -> p j n", j=2),
                               in_=maskadd.rearrange("j p n -> p j n")).then_inc(ld, 16)
                sync.dma_start(out=conv_in[:, :].rearrange("p (t z) -> p t z", t=34)[:, :, 0:4],
                               in_=halo_in.rearrange("t p z -> p t z")).then_inc(ld, 16)
                c_.inc("ld", 16 * 8)
                sync.wait_ge(ld, c_.val("ld"))

            @block.vector
            def _(vector):
                vector.memset(onesrow64[:, :], 1.0)
                vector.memset(onesf[:, :], 1.0)

        dtb_col = dts[:, 2560 - 256:2560 - 255]
        alog_col = dts[:, 2560 - 255:2560 - 254]
        negA_col = dts[:, 2560 - 254:2560 - 253]

        with nc.Block() as block:
            @block.scalar
            def _(scalar):
                scalar.activation(negA_col, alog_col, AF.Exp)
        with nc.Block() as block:
            @block.vector
            def _(vector):
                vector.tensor_scalar(negA_col, negA_col, scalar1=-1.0, scalar2=None, op0=ALU.mult)

        SL = lambda i: dts[:, i * 256:(i + 1) * 256]

        for c in range(NCH):
            with nc.Block() as block:
                @block.sync
                def _(sync):
                    sync.dma_start(
                        out=conv_in[:, :].rearrange("p (t z) -> p t z", t=34)[:, :, 4:260],
                        in_=zx[c, 32:66].rearrange("m p t -> p m t")).then_inc(ld, 16)
                    sync.dma_start(out=dtbuf[:, :], in_=zx[c, 66, 0:64, :]).then_inc(ld, 16)
                    c_.inc("ld", 32)
                    sync.wait_ge(ld, c_.val("ld"))

            if 'conv' not in stages:
                continue
            with nc.Block() as block:
                @block.vector
                def _(vector):
                    vector.tensor_scalar(SL(0), dtbuf[:, :], scalar1=dtb_col,
                                         scalar2=None, op0=ALU.add)
                    vector.tensor_scalar(SL(1), SL(0), scalar1=0.0, scalar2=None, op0=ALU.max)

            as3 = c_.val("as_")
            with nc.Block() as block:
                @block.vector
                def _(vector):
                    for tt in range(34):
                        sl = tt % 2
                        if tt >= 2:
                            vector.wait_ge(as_, as3 + tt)
                        a = cacc[:, sl * 256:(sl + 1) * 256]
                        ci = lambda o: conv_in[:, tt * 260 + o:tt * 260 + o + 256]
                        vector.tensor_scalar(a, ci(1), scalar1=convw_sb[:, tt * 4 + 0:tt * 4 + 1],
                                             scalar2=None, op0=ALU.mult)
                        vector.scalar_tensor_tensor(a, ci(2), convw_sb[:, tt * 4 + 1:tt * 4 + 2], a,
                                                    op0=ALU.mult, op1=ALU.add)
                        vector.scalar_tensor_tensor(a, ci(3), convw_sb[:, tt * 4 + 2:tt * 4 + 3], a,
                                                    op0=ALU.mult, op1=ALU.add)
                        ins = vector.scalar_tensor_tensor(a, ci(4), convw_sb[:, tt * 4 + 3:tt * 4 + 4], a,
                                                          op0=ALU.mult, op1=ALU.add)
                        ins.then_inc(vs, 1)
                        c_.inc("vs")

                @block.scalar
                def _(scalar):
                    scalar.activation(SL(2), SL(0), AF.Abs)
                    scalar.activation(SL(2), SL(2), AF.Exp, scale=-1.0)
                    scalar.activation(SL(2), SL(2), AF.Ln, bias=1.0).then_inc(as_, 1)
                    c_.inc("as_")
                    vbase = c_.val("vs") - 34
                    for tt in range(34):
                        scalar.wait_ge(vs, vbase + tt + 1)
                        sl = tt % 2
                        scalar.activation(xconv[:, tt * 256:(tt + 1) * 256],
                                          cacc[:, sl * 256:(sl + 1) * 256], AF.Silu,
                                          bias=convb_sb[:, tt:tt + 1]).then_inc(as_, 1)
                        c_.inc("as_")

            if 'dt' not in stages:
                continue
            with nc.Block() as block:
                @block.vector
                def _(vector):
                    vector.tensor_tensor(SL(4), SL(1), SL(2), op=ALU.add).then_inc(vs, 1)
                    c_.inc("vs")
                    vector.tensor_scalar(SL(5), SL(4), scalar1=negA_col, scalar2=None, op0=ALU.mult)
                    vector.tensor_tensor_scan(dacs[:, :], onesrow64[:, :], SL(5), 0.0,
                                              op0=ALU.mult, op1=ALU.add).then_inc(vs, 1)
                    c_.inc("vs")

                @block.scalar
                def _(scalar):
                    scalar.wait_ge(vs, c_.val("vs") - 1)
                    scalar.activation(SL(3), SL(4), AF.Ln)
                    scalar.wait_ge(vs, c_.val("vs"))
                    scalar.activation(SL(6), dacs[:, :], AF.Exp, scale=-1.0,
                                      bias=dacs[:, 255:256])
                    scalar.activation(cdcol[:, :], dacs[:, 255:256], AF.Exp)

            if 'tr' not in stages:
                continue
            with nc.Block() as block:
                @block.vector
                def _(vector):
                    vector.tensor_tensor(SL(7), SL(3), dacs[:, :], op=ALU.subtract)
                    vector.tensor_tensor(SL(8), SL(6), SL(4), op=ALU.mult)
                    nb = c_.val("ts")
                    for g in range(8):
                        vector.wait_ge(ts, nb + g + 1)
                        st8 = (g % 2) * 1024
                        dst_st = 0 if g < 4 else 1
                        dst_pp = (g % 4) * 8
                        vector.tensor_copy(
                            xcT[:, dst_st * 4096 + dst_pp * 128: dst_st * 4096 + (dst_pp + 8) * 128],
                            pbf[:, st8:st8 + 1024]).then_inc(vs, 1)
                        c_.inc("vs")
                    vector.wait_ge(ts, nb + 9)
                    vector.tensor_copy(bT[:, :], pbf[:, 0:256]).then_inc(vs, 1)
                    c_.inc("vs")

                @block.tensor
                def _(tensor):
                    vb = c_.val("vs") - 9
                    for g in range(8):
                        st = 0 if g < 4 else 1
                        if g >= 2:
                            tensor.wait_ge(vs, vb + g - 1)
                        for k in range(8):
                            pp = (g % 4) * 8 + k
                            ins = tensor.transpose(pbf[:, (g % 2) * 1024 + k * 128:(g % 2) * 1024 + (k + 1) * 128],
                                             xconv[:, pp * 256 + st * 128:pp * 256 + st * 128 + 128],
                                             idb[:, :])
                        ins.then_inc(ts, 1)
                        c_.inc("ts")
                    tensor.wait_ge(vs, vb + 8)
                    for st in range(2):
                        ins = tensor.transpose(pbf[:, st * 128:(st + 1) * 128],
                                         xconv[:, 32 * 256 + st * 128:32 * 256 + st * 128 + 128],
                                         idb[:, :])
                    ins.then_inc(ts, 1)
                    c_.inc("ts")

            if 'cb' not in stages:
                continue
            ts0 = c_.val("ts")
            vs0 = c_.val("vs")
            with nc.Block() as block:
                @block.tensor
                def _(tensor):
                    for st in range(2):
                        tensor.transpose(big1[:, st * 64:(st + 1) * 64],
                                         SL(7)[:, st * 128:(st + 1) * 128], id64[:, :])
                        tensor.transpose(big1[:, 128 + st * 64:128 + (st + 1) * 64],
                                         SL(8)[:, st * 128:(st + 1) * 128], id64[:, :])
                    ins = tensor.transpose(big1[0:1, 256:320], cdcol[0:64, 0:1], id64[:, :])
                    ins.then_inc(ts, 1)
                    tensor.wait_ge(vs, vs0 + 1)
                    tensor.matmul(big1[:, 512:576], onesf[:, :], cdrow[:, :],
                                  start=True, stop=True).then_inc(ts, 1)
                    for r in range(8):
                        ins = tensor.matmul(big0[:, r * 256:(r + 1) * 256],
                                      xconv[:, 32 * 256:32 * 256 + 128],
                                      xconv[:, 33 * 256:34 * 256],
                                      start=True, stop=True)
                    ins.then_inc(ts, 1)

                c_.inc("ts", 3)

                @block.vector
                def _(vector):
                    vector.wait_ge(ts, ts0 + 1)
                    for st in range(2):
                        vector.tensor_copy(cmbT[:, st * 64:(st + 1) * 64],
                                           big1[:, st * 64:(st + 1) * 64])
                        vector.tensor_copy(wT[:, st * 64:(st + 1) * 64],
                                           big1[:, 128 + st * 64:128 + (st + 1) * 64])
                    vector.tensor_copy(cdrow[0:1, 0:64], big1[0:1, 256:320]).then_inc(vs, 1)
                    vector.wait_ge(ts, ts0 + 2)
                    vector.tensor_copy(cd_u[:, :], big1[:, 512:576]).then_inc(vs, 1)
                    vector.wait_ge(ts, ts0 + 3)
                    vector.tensor_copy(cbm[:, 0:2048], big0[:, :]).then_inc(vs, 1)

                c_.inc("vs", 3)

                @block.gpsimd
                def _(gpsimd):
                    if c < NCH - 1:
                        for tt in range(34):
                            gpsimd.tensor_copy(conv_in[:, tt * 260:tt * 260 + 4],
                                               conv_in[:, tt * 260 + 4 + 252:tt * 260 + 4 + 256])

                @block.sync
                def _(sync):
                    sync.dma_start(out=dab[c], in_=dacs[:, :]).then_inc(dw, 16)
                    c_.inc("dw", 16)
                    sync.dma_start(out=dacs_out[c], in_=dacs[:, :]).then_inc(so, 16)
                    sync.dma_start(out=cfeat_out[c], in_=xconv[:, 33 * 256:34 * 256]).then_inc(so, 16)
                    c_.inc("so", 32)

            with nc.Block() as block:
                @block.tensor
                def _(tensor):
                    for r in range(8):
                        ins = tensor.matmul(big0[:, r * 256:(r + 1) * 256],
                                      xconv[:, 32 * 256 + 128:32 * 256 + 256],
                                      xconv[:, 33 * 256:34 * 256],
                                      start=True, stop=True)
                    ins.then_inc(ts, 1)
                    c_.inc("ts")

                @block.vector
                def _(vector):
                    vector.wait_ge(ts, c_.val("ts"))
                    vector.tensor_copy(cbm[:, 2048:4096], big0[:, :]).then_inc(vs, 1)
                    c_.inc("vs")

            for jt in (range(2) if 'lm' in stages else []):
                for hg in range(8):
                    it7 = c * 16 + jt * 8 + hg
                    with nc.Block() as block:
                        @block.sync
                        def _(sync):
                            sync.wait_ge(dw, c_.val("dw"))
                            sync.dma_start(
                                out=dabc[:, :],
                                in_=bass.AP(dab.tensor, c * 64 * 256 + hg * 8 * 256,
                                            [[0, 128], [1, 2048]])).then_inc(bc, 16)
                            c_.inc("bc", 16)

                        @block.vector
                        def _(vector):
                            vector.wait_ge(bc, c_.val("bc"))
                            for h8 in range(8):
                                h = hg * 8 + h8
                                ins = vector.scalar_tensor_tensor(
                                    argb[:, h8 * 256:(h8 + 1) * 256],
                                    dabc[:, h8 * 256:(h8 + 1) * 256],
                                    cmbT[:, jt * 64 + h:jt * 64 + h + 1],
                                    maska[:, jt * 2048 + h8 * 256:jt * 2048 + (h8 + 1) * 256],
                                    op0=ALU.add, op1=ALU.add)
                            ins.then_inc(vs, 1)
                            c_.inc("vs")

                        @block.scalar
                        def _(scalar):
                            scalar.wait_ge(vs, c_.val("vs"))
                            scalar.activation(lm[:, :], argb[:, :], AF.Exp).then_inc(as_, 1)
                            c_.inc("as_")

                        @block.vector
                        def _(vector):
                            vector.wait_ge(as_, c_.val("as_"))
                            vector.tensor_tensor(mbuf[:, :], lm[:, :],
                                                 cbm[:, jt * 2048:(jt + 1) * 2048],
                                                 op=ALU.mult).then_inc(vs, 1)
                            c_.inc("vs")

                        @block.sync
                        def _(sync):
                            sync.wait_ge(vs, c_.val("vs"))
                            sync.dma_start(out=m_spill[c, jt, hg], in_=mbuf[:, :]).then_inc(msp, 16)
                            c_.inc("msp", 16)
                            sync.wait_ge(msp, c_.val("msp"))

            if 'st' not in stages:
                continue
            ts1 = c_.val("ts")
            vs1 = c_.val("vs")
            as1 = c_.val("as_")
            with nc.Block() as block:
                @block.vector
                def _(vector):
                    for pp in range(32):
                        if pp >= 2:
                            vector.wait_ge(ts, ts1 + pp - 1)
                        so8 = (pp % 2) * 4
                        for q in range(2):
                            h = 2 * pp + q
                            for st in range(2):
                                ins = vector.tensor_scalar(
                                    bwT[:, (so8 + 2 * q + st) * 128:(so8 + 2 * q + st + 1) * 128],
                                    bT[:, st * 128:(st + 1) * 128],
                                    scalar1=wT[:, st * 64 + h:st * 64 + h + 1],
                                    scalar2=None, op0=ALU.mult)
                        ins.then_inc(vs, 1)

                @block.tensor
                def _(tensor):
                    for pp in range(32):
                        tensor.wait_ge(vs, vs1 + pp + 1)
                        if pp >= 2:
                            tensor.wait_ge(as_, as1 + pp - 1)
                        sl = pp % 2
                        so8 = (pp % 2) * 4
                        for q in range(2):
                            h = 2 * pp + q
                            for st in range(2):
                                ins = tensor.matmul(big1[:, sl * 512 + q * 64: sl * 512 + q * 64 + 64],
                                              bwT[:, (so8 + 2 * q + st) * 128:(so8 + 2 * q + st + 1) * 128],
                                              xcT[:, st * 4096 + h * 64:st * 4096 + (h + 1) * 64],
                                              start=(st == 0), stop=(st == 1))
                        ins.then_inc(ts, 1)

                @block.scalar
                def _(scalar):
                    for pp in range(32):
                        scalar.wait_ge(ts, ts1 + pp + 1)
                        sl = pp % 2
                        scalar.copy(states[:, pp * 128:(pp + 1) * 128],
                                    big1[:, sl * 512:sl * 512 + 128]).then_inc(as_, 1)

            c_.inc("vs", 32)
            c_.inc("ts", 32)
            c_.inc("as_", 32)

            with nc.Block() as block:
                @block.vector
                def _(vector):
                    if c == 0:
                        vector.tensor_copy(carry[:, :], states[:, :])
                    else:
                        for h in range(64):
                            vector.scalar_tensor_tensor(
                                carry[:, h * 64:(h + 1) * 64], carry[:, h * 64:(h + 1) * 64],
                                cd_u[:, h:h + 1], states[:, h * 64:(h + 1) * 64],
                                op0=ALU.mult, op1=ALU.add)

            with nc.Block() as block:
                @block.sync
                def _(sync):
                    sync.dma_start(out=st_spill[c], in_=states[:, :]).then_inc(so, 16)
                    sync.dma_start(out=xcT_spill[c, 0], in_=xcT[:, 0:4096]).then_inc(so, 16)
                    sync.dma_start(out=xcT_spill[c, 1], in_=xcT[:, 4096:8192]).then_inc(so, 16)
                    c_.inc("so", 48)

        with nc.Block() as block:
            @block.sync
            def _(sync):
                sync.dma_start(out=carry_out, in_=carry[:, :]).then_inc(so, 16)
                c_.inc("so", 16)
                sync.wait_ge(so, c_.val("so"))
                sync.wait_ge(msp, c_.val("msp"))
    return nc


def _prep_l2a(inputs, l1_results):
    conv_w = np.asarray(inputs["L0_attn_conv_w"])
    conv_b = np.asarray(inputs["L0_attn_conv_b"])
    dt_bias = np.asarray(inputs["L0_attn_dt_bias"])
    A_log = np.asarray(inputs["L0_attn_A_log"])

    convw = np.ascontiguousarray(conv_w.reshape(34, 128, 4).astype(np.float32))
    convb = np.ascontiguousarray(conv_b.reshape(34, 128).T.astype(np.float32))
    dtb = dt_bias.reshape(64, 1).astype(np.float32)
    alog = A_log.reshape(64, 1).astype(np.float32)
    j_idx = np.arange(128)[:, None]
    i_idx = np.arange(256)[None, :]
    maskadd = np.zeros((2, 128, 2048), dtype=np.float32)
    for jt in range(2):
        m = (i_idx >= jt * 128 + j_idx).astype(np.float32)
        maskadd[jt] = np.tile((1.0 - m) * -1e30, (1, 8))
    id64f = np.eye(64, dtype=np.float32)
    idbf = np.eye(128, dtype=bf16)

    in_maps = []
    for c in range(NCORE):
        h = c % 2
        zx_c = np.asarray(l1_results[c]["zx_spill"]).reshape(NCH, 67, 128, CHUNK)
        if h == 0:
            halo = np.zeros((34, 128, 4), dtype=np.float32)
        else:
            halo = np.ascontiguousarray(
                np.asarray(l1_results[c - 1]["zx_spill"]).reshape(NCH, 67, 128, CHUNK)[3, 32:66, :, 252:256]).astype(np.float32)
        in_maps.append({
            "zx": zx_c, "halo_in": halo, "convw": convw, "convb": convb,
            "dtbias": dtb, "alog": alog, "maskadd": maskadd,
            "id64f": id64f, "idbf": idbf,
        })
    return in_maps



def build_l2b():
    nc = _mk_nc()
    zx = nc.dram_tensor("zx", [NCH, 67, 128, CHUNK], F32, kind="ExternalInput").ap()
    m_in = nc.dram_tensor("m_in", [NCH, 2, 8, 128, 2048], BF16, kind="ExternalInput").ap()
    xcT_in = nc.dram_tensor("xcT_in", [NCH, 2, 128, 4096], BF16, kind="ExternalInput").ap()
    st_in = nc.dram_tensor("st_in", [NCH, 128, 4096], BF16, kind="ExternalInput").ap()
    dacs_in = nc.dram_tensor("dacs_in", [NCH, 64, 256], F32, kind="ExternalInput").ap()
    cfeat_in = nc.dram_tensor("cfeat_in", [NCH, 128, 256], BF16, kind="ExternalInput").ap()
    carry_in = nc.dram_tensor("carry_in", [128, 4096], F32, kind="ExternalInput").ap()
    wo_in = nc.dram_tensor("wo_in", [16, 32, 128, 128], F32, kind="ExternalInput").ap()
    nw_in = nc.dram_tensor("nw_in", [128, 32], F32, kind="ExternalInput").ap()
    resid_in = nc.dram_tensor("resid_in", [KT, 128, TC], F32, kind="ExternalInput").ap()
    dprow_in = nc.dram_tensor("dprow_in", [1, 64], F32, kind="ExternalInput").ap()
    id64f = nc.dram_tensor("id64f", [64, 64], F32, kind="ExternalInput").ap()
    idbf = nc.dram_tensor("idbf", [128, 128], BF16, kind="ExternalInput").ap()
    idff = nc.dram_tensor("idff", [128, 128], F32, kind="ExternalInput").ap()

    r1_out = nc.dram_tensor("r1_out", [KT, 128, TC], BF16, kind="ExternalOutput").ap()

    vsp = nc.dram_tensor("vsp", [KT, 128, TC], F32).ap()

    c_ = CT()
    with ExitStack() as stk:
        sb = lambda name, shape, dt: stk.enter_context(nc.sbuf_tensor(name, shape, dt))
        sem = lambda name: stk.enter_context(nc.semaphore(name))
        id64 = sb("id64", [64, 64], F32)
        idb = sb("idb", [128, 128], BF16)
        idf = sb("idf", [128, 128], F32)
        nw = sb("nw", [128, 32], F32)
        obf = sb("obf", [128, 1], BF16)
        onesf = sb("onesf", [1, 128], F32)
        scl = sb("scl", [1, 4], F32)
        dacs_sb = sb("dacs_sb", [64, 4 * 256], F32)
        cfeat = sb("cfeat", [128, 4 * 256], BF16)
        eaT = sb("eaT", [128, 8 * 64], F32)
        cdrows = sb("cdrows", [1, 4 * 64], F32)
        dprows = sb("dprows", [1, 4 * 64], F32)
        cdu = sb("cdu", [128, 4 * 64], F32)
        dpu = sb("dpu", [128, 64], F32)
        dprow2 = sb("dprow2", [1, 64], F32)
        ea_r = sb("ea_r", [64, 256], F32)
        cdc = sb("cdc", [64, 1], F32)
        rstd_b = sb("rstd_b", [128, 1024], F32)
        ab_b = sb("ab_b", [128, 2 * 1024], F32)
        row_sc = sb("row_sc", [1, 4 * 1024], F32)
        yg = sb("yg", [128, 32 * 1024], BF16)
        ld = sem("ld"); ldg = sem("ldg"); ts = sem("ts"); vs = sem("vs")
        as_ = sem("as_"); so = sem("so"); msp = sem("msp"); gsm = sem("gsm")
        wsl = [sem("wslA"), sem("wslB")]
        rsl = [sem("rslA"), sem("rslB")]
        big0 = stk.enter_context(nc.psum_tensor("big0", [128, 2048], F32))
        big1 = stk.enter_context(nc.psum_tensor("big1", [128, 1024], F32))
        pbf = stk.enter_context(nc.psum_tensor("pbf", [128, 1024], BF16))

        with nc.Block() as block:
            @block.sync
            def _(sync):
                sync.dma_start(out=id64[:, :], in_=id64f).then_inc(ld, 16)
                sync.dma_start(out=idb[:, :], in_=idbf).then_inc(ld, 16)
                sync.dma_start(out=idf[:, :], in_=idff).then_inc(ld, 16)
                sync.dma_start(out=nw[:, :], in_=nw_in).then_inc(ld, 16)
                sync.dma_start(out=dprow2[:, :], in_=dprow_in).then_inc(ld, 16)
                sync.dma_start(out=dacs_sb[:, :].rearrange("p (c t) -> p c t", c=4),
                               in_=dacs_in.rearrange("c p t -> p c t")).then_inc(ld, 16)
                sync.dma_start(out=cfeat[:, :].rearrange("p (c t) -> p c t", c=4),
                               in_=cfeat_in.rearrange("c p t -> p c t")).then_inc(ld, 16)
                c_.inc("ld", 16 * 7)
                sync.wait_ge(ld, c_.val("ld"))

            @block.vector
            def _(vector):
                vector.memset(obf[:, :], 1.0)
                vector.memset(onesf[:, :], 1.0)
                vector.memset(scl[0:1, 0:1], 1.0 / 4096.0)
                vector.memset(scl[0:1, 1:2], 1e-5)
                vector.memset(scl[0:1, 2:3], 1.0 / 2048.0)
                vector.memset(dprows[0:1, 0:64], 1.0)

        for c in range(NCH):
            with nc.Block() as block:
                @block.scalar
                def _(scalar):
                    scalar.activation(ea_r[:, :], dacs_sb[:, c * 256:(c + 1) * 256], AF.Exp)
                    scalar.activation(cdc[:, :], dacs_sb[:, c * 256 + 255:c * 256 + 256], AF.Exp).then_inc(as_, 1)
                    c_.inc("as_")

                @block.tensor
                def _(tensor):
                    tensor.wait_ge(as_, c_.val("as_"))
                    for it in range(2):
                        tensor.transpose(big1[:, it * 512:it * 512 + 64],
                                         ea_r[:, it * 128:(it + 1) * 128], id64[:, :])
                    ins = tensor.transpose(big1[0:1, 64:128], cdc[0:64, 0:1], id64[:, :])
                    ins.then_inc(ts, 1)
                    c_.inc("ts")

                @block.vector
                def _(vector):
                    vector.wait_ge(ts, c_.val("ts"))
                    for it in range(2):
                        vector.tensor_copy(eaT[:, (c * 2 + it) * 64:(c * 2 + it + 1) * 64],
                                           big1[:, it * 512:it * 512 + 64])
                    vector.tensor_copy(cdrows[0:1, c * 64:(c + 1) * 64], big1[0:1, 64:128]).then_inc(vs, 1)
                    c_.inc("vs")

        for c in range(1, NCH):
            with nc.Block() as block:
                @block.vector
                def _(vector):
                    vector.tensor_tensor(dprows[0:1, c * 64:(c + 1) * 64],
                                         dprows[0:1, (c - 1) * 64:c * 64],
                                         cdrows[0:1, (c - 1) * 64:c * 64], op=ALU.mult)

        with nc.Block() as block:
            @block.tensor
            def _(tensor):
                for c in range(NCH):
                    tensor.matmul(big0[:, c * 64:(c + 1) * 64], onesf[:, :],
                                  cdrows[0:1, c * 64:(c + 1) * 64], start=True, stop=True)
                for c in range(NCH):
                    tensor.matmul(big0[:, 1024 + c * 64:1024 + (c + 1) * 64], onesf[:, :],
                                  dprows[0:1, c * 64:(c + 1) * 64], start=True, stop=True)
                ins = tensor.matmul(big1[:, 0:64], onesf[:, :], dprow2[:, :], start=True, stop=True)
                ins.then_inc(ts, 1)
                c_.inc("ts")

            @block.vector
            def _(vector):
                vector.wait_ge(ts, c_.val("ts"))
                vector.tensor_copy(cdu[:, :], big0[:, 0:256])
                vector.tensor_copy(dpu[:, :], big1[:, 0:64])

        with ExitStack() as stk2:
            sb2 = lambda name, shape, dt: stk2.enter_context(nc.sbuf_tensor(name, shape, dt))
            dpi = sb2("dpi", [128, 8 * 128], BF16)
            prev = sb2("prev", [128, 4096], F32)
            prevbf = sb2("prevbf", [128, 4096], BF16)
            states = sb2("states", [128, 4096], BF16)
            xcT = sb2("xcT", [128, 2 * 4096], BF16)
            m01 = sb2("m01", [128, 2 * 2048], BF16)
            zsil = sb2("zsil", [128, 32 * 256], F32)
            ysb = sb2("ysb", [128, 1024], F32)

            with nc.Block() as block:
                @block.sync
                def _(sync):
                    sync.dma_start(out=prev[:, :], in_=carry_in).then_inc(ld, 16)
                    c_.inc("ld", 16)
                    sync.wait_ge(ld, c_.val("ld"))

                @block.gpsimd
                def _(gpsimd):
                    pass

            with nc.Block() as block:
                @block.vector
                def _(vector):
                    vector.tensor_copy(prevbf[:, :], prev[:, :])

            for c in range(NCH):
                with nc.Block() as block:
                    @block.sync
                    def _(sync):
                        sync.dma_start(out=xcT[:, :].rearrange("p (s n) -> p s n", s=2),
                                       in_=xcT_in[c].rearrange("s p n -> p s n")).then_inc(ld, 16)
                        sync.dma_start(out=states[:, :], in_=st_in[c]).then_inc(ld, 16)
                        sync.dma_start(out=zsil[:, :].rearrange("p (m t) -> p m t", m=32),
                                       in_=zx[c, 0:32].rearrange("m p t -> p m t")).then_inc(ld, 16)
                        c_.inc("ld", 48)
                        sync.wait_ge(ld, c_.val("ld"))

                    @block.scalar
                    def _(scalar):
                        pass

                with nc.Block() as block:
                    @block.scalar
                    def _(scalar):
                        for pp in range(32):
                            scalar.activation(zsil[:, pp * 256:(pp + 1) * 256],
                                              zsil[:, pp * 256:(pp + 1) * 256], AF.Silu)

                for hg in range(8):
                    with nc.Block() as block:
                        @block.sync
                        def _(sync):
                            sync.dma_start(out=m01[:, 0:2048], in_=m_in[c, 0, hg]).then_inc(ld, 16)
                            sync.dma_start(out=m01[:, 2048:4096], in_=m_in[c, 1, hg]).then_inc(ld, 16)
                            c_.inc("ld", 32)
                            sync.wait_ge(ld, c_.val("ld"))

                        @block.gpsimd
                        def _(gpsimd):
                            for h8 in range(8):
                                h = hg * 8 + h8
                                ins = gpsimd.tensor_scalar(dpi[:, h8 * 128:(h8 + 1) * 128], idb[:, :],
                                                     scalar1=dpu[:, h:h + 1], scalar2=None,
                                                     op0=ALU.mult)
                            ins.then_inc(gsm, 1)
                            c_.inc("gsm")

                        @block.tensor
                        def _(tensor):
                            tensor.wait_ge(ld, c_.val("ld"))
                            tensor.wait_ge(gsm, c_.val("gsm"))
                            for it in range(2):
                                for h8 in range(8):
                                    h = hg * 8 + h8
                                    dcol = it * 512 + h8 * 64
                                    tensor.matmul(big0[:, dcol:dcol + 64],
                                                  m01[:, 0 * 2048 + h8 * 256 + it * 128:0 * 2048 + h8 * 256 + it * 128 + 128],
                                                  xcT[:, 0 * 4096 + h * 64:0 * 4096 + (h + 1) * 64],
                                                  start=True, stop=False)
                                    if it == 1:
                                        tensor.matmul(big0[:, dcol:dcol + 64],
                                                      m01[:, 1 * 2048 + h8 * 256 + 128:1 * 2048 + h8 * 256 + 256],
                                                      xcT[:, 1 * 4096 + h * 64:1 * 4096 + (h + 1) * 64],
                                                      start=False, stop=False)
                                    tensor.matmul(big0[:, dcol:dcol + 64],
                                                  dpi[:, h8 * 128:(h8 + 1) * 128],
                                                  xcT[:, it * 4096 + h * 64:it * 4096 + (h + 1) * 64],
                                                  start=False, stop=True)
                                    ins = tensor.matmul(big0[:, 1024 + dcol:1024 + dcol + 64],
                                                  cfeat[:, c * 256 + it * 128:c * 256 + it * 128 + 128],
                                                  prevbf[:, h * 64:(h + 1) * 64],
                                                  start=True, stop=True)
                            ins.then_inc(ts, 1)
                            c_.inc("ts")

                        @block.scalar
                        def _(scalar):
                            scalar.wait_ge(ts, c_.val("ts"))
                            for it in range(2):
                                for h8 in range(8):
                                    h = hg * 8 + h8
                                    dcol = it * 512 + h8 * 64
                                    ins = scalar.activation(
                                        ysb[:, dcol:dcol + 64], big0[:, 1024 + dcol:1024 + dcol + 64],
                                        AF.Copy, scale=eaT[:, (c * 2 + it) * 64 + h:(c * 2 + it) * 64 + h + 1])
                            ins.then_inc(as_, 1)
                            c_.inc("as_")

                        @block.vector
                        def _(vector):
                            vector.wait_ge(as_, c_.val("as_"))
                            vector.tensor_tensor(ysb[:, :], ysb[:, :], big0[:, 0:1024],
                                                 op=ALU.add).then_inc(vs, 1)
                            c_.inc("vs")

                        @block.tensor
                        def _(tensor):
                            tensor.wait_ge(vs, c_.val("vs"))
                            for k in range(8):
                                ins = tensor.transpose(big1[:, k * 128:(k + 1) * 128],
                                                       ysb[:, k * 128:(k + 1) * 128], idf[:, :])
                            ins.then_inc(ts, 1)
                            c_.inc("ts")

                        @block.vector
                        def _(vector):
                            vector.wait_ge(ts, c_.val("ts"))
                            for k in range(8):
                                it, pr = k // 4, k % 4
                                pp = hg * 4 + pr
                                vector.tensor_tensor(
                                    yg[:, pp * 1024 + c * 256 + it * 128:pp * 1024 + c * 256 + it * 128 + 128],
                                    big1[:, k * 128:(k + 1) * 128],
                                    zsil[:, pp * 256 + it * 128:pp * 256 + it * 128 + 128],
                                    op=ALU.mult)
                            vector.tensor_scalar(ysb[:, 0:1], ysb[:, 0:1], scalar1=0.0,
                                                 scalar2=None, op0=ALU.add).then_inc(vs, 1)
                            c_.inc("vs")

                with nc.Block() as block:
                    @block.vector
                    def _(vector):
                        for h in range(64):
                            vector.scalar_tensor_tensor(
                                prev[:, h * 64:(h + 1) * 64], prev[:, h * 64:(h + 1) * 64],
                                cdu[:, c * 64 + h:c * 64 + h + 1], states[:, h * 64:(h + 1) * 64],
                                op0=ALU.mult, op1=ALU.add)
                        vector.tensor_copy(prevbf[:, :], prev[:, :])

        with ExitStack() as stk3:
            sb3 = lambda name, shape, dt: stk3.enter_context(nc.sbuf_tensor(name, shape, dt))
            sqb = sb3("sqb", [128, 2 * 1024], BF16)
            wobuf = sb3("wobuf", [128, 2 * 32 * 128], BF16)
            resid = sb3("resid", [128, 2 * 1024], F32)
            vbuf = sb3("vbuf", [128, 2 * 1024], F32)
            vbf = sb3("vbf", [128, 2 * 1024], BF16)
            r1b = sb3("r1b", [128, 2 * 1024], BF16)

            ts_r = c_.val("ts"); as_r = c_.val("as_")
            with nc.Block() as block:
                @block.scalar
                def _(scalar):
                    for pp in range(32):
                        if pp >= 2:
                            scalar.wait_ge(ts, ts_r + pp - 1)
                        scalar.activation(sqb[:, (pp % 2) * 1024:(pp % 2 + 1) * 1024],
                                          yg[:, pp * 1024:(pp + 1) * 1024],
                                          AF.Square).then_inc(as_, 1)

                @block.tensor
                def _(tensor):
                    for pp in range(32):
                        tensor.wait_ge(as_, as_r + pp + 1)
                        for n in range(2):
                            ins = tensor.matmul(
                                big1[0:1, n * 512:(n + 1) * 512], obf[:, :],
                                sqb[:, (pp % 2) * 1024 + n * 512:(pp % 2) * 1024 + (n + 1) * 512],
                                start=(pp == 0), stop=(pp == 31))
                        ins.then_inc(ts, 1)
            c_.inc("as_", 32)
            c_.inc("ts", 32)

            with nc.Block() as block:
                @block.vector
                def _(vector):
                    for pp in range(32):
                        vector.tensor_scalar(yg[:, pp * 1024:(pp + 1) * 1024],
                                             yg[:, pp * 1024:(pp + 1) * 1024],
                                             scalar1=nw[:, pp:pp + 1], scalar2=None,
                                             op0=ALU.mult)

            with nc.Block() as block:
                @block.scalar
                def _(scalar):
                    scalar.activation(row_sc[0:1, 0:1024], big1[0:1, 0:1024], AF.Sqrt,
                                      scale=scl[0:1, 0:1], bias=scl[0:1, 1:2])
            with nc.Block() as block:
                @block.vector
                def _(vector):
                    vector.reciprocal(row_sc[0:1, 0:1024], row_sc[0:1, 0:1024])
            with nc.Block() as block:
                @block.tensor
                def _(tensor):
                    for n in range(2):
                        ins = tensor.matmul(big0[:, n * 512:(n + 1) * 512], onesf[:, :],
                                            row_sc[0:1, n * 512:(n + 1) * 512], start=True, stop=True)
                    ins.then_inc(ts, 1)
                    c_.inc("ts")

                @block.vector
                def _(vector):
                    vector.wait_ge(ts, c_.val("ts"))
                    vector.tensor_copy(rstd_b[:, :], big0[:, 0:1024])

            ts3 = c_.val("ts"); vs3 = c_.val("vs")
            with nc.Block() as block:
                @block.gpsimd
                def _(gpsimd):
                    for m in range(16):
                        if m >= 2:
                            gpsimd.wait_ge(vs, vs3 + m - 1)
                        sl = m % 2
                        gpsimd.dma_start(
                            out=wobuf[:, sl * 4096:(sl + 1) * 4096].rearrange("p (k mc) -> p k mc", k=32),
                            in_=wo_in[m].rearrange("k p mc -> p k mc")).then_inc(wsl[sl], 16)

                @block.sync
                def _(sync):
                    for m in range(16):
                        if m >= 2:
                            sync.wait_ge(vs, vs3 + m - 1)
                        sl = m % 2
                        sync.dma_start(out=resid[:, sl * 1024:(sl + 1) * 1024],
                                       in_=resid_in[m]).then_inc(rsl[sl], 16)

                @block.tensor
                def _(tensor):
                    for m in range(16):
                        sl = m % 2
                        tensor.wait_ge(wsl[sl], 16 * (m // 2 + 1))
                        for n in range(2):
                            u = 2 * m + n
                            if u >= 4:
                                tensor.wait_ge(vs, vs3 + (u // 2) - 1)
                            for k in range(32):
                                ins = tensor.matmul(
                                    big0[:, (u % 4) * 512:(u % 4 + 1) * 512],
                                    wobuf[:, sl * 4096 + k * 128:sl * 4096 + (k + 1) * 128],
                                    yg[:, k * 1024 + n * 512:k * 1024 + (n + 1) * 512],
                                    start=(k == 0), stop=(k == 31))
                            ins.then_inc(ts, 1)
                        tensor.wait_ge(vs, vs3 + m + 1)
                        for n in range(2):
                            tensor.matmul(big1[0:1, n * 512:(n + 1) * 512], obf[:, :],
                                          vbf[:, sl * 1024 + n * 512:sl * 1024 + (n + 1) * 512],
                                          start=(m == 0), stop=(m == 15))

                @block.vector
                def _(vector):
                    for m in range(16):
                        sl = m % 2
                        vector.wait_ge(rsl[sl], 16 * (m // 2 + 1))
                        if m >= 2:
                            vector.wait_ge(so, 16 * (m - 1))
                        for n in range(2):
                            u = 2 * m + n
                            vector.wait_ge(ts, ts3 + 2 * m + n + 1)
                            vector.tensor_tensor(vbuf[:, sl * 1024 + n * 512:sl * 1024 + (n + 1) * 512],
                                                 big0[:, (u % 4) * 512:(u % 4 + 1) * 512],
                                                 rstd_b[:, n * 512:(n + 1) * 512], op=ALU.mult)
                            vector.tensor_tensor(vbuf[:, sl * 1024 + n * 512:sl * 1024 + (n + 1) * 512],
                                                 vbuf[:, sl * 1024 + n * 512:sl * 1024 + (n + 1) * 512],
                                                 resid[:, sl * 1024 + n * 512:sl * 1024 + (n + 1) * 512],
                                                 op=ALU.add)
                        vector.tensor_copy(vbf[:, sl * 1024:(sl + 1) * 1024],
                                           vbuf[:, sl * 1024:(sl + 1) * 1024]).then_inc(vs, 1)

                @block.scalar
                def _(scalar):
                    for m in range(16):
                        sl = m % 2
                        scalar.wait_ge(vs, vs3 + m + 1)
                        scalar.dma_start(out=vsp[m], in_=vbuf[:, sl * 1024:(sl + 1) * 1024]).then_inc(so, 16)

            c_.inc("ts", 16 * 2)
            c_.inc("vs", 16)
            c_.inc("so", 16 * 16)

            with nc.Block() as block:
                @block.vector
                def _(vector):
                    vector.tensor_copy(row_sc[0:1, 1024:2048], big1[0:1, 0:1024])

            as4 = c_.val("as_")
            with nc.Block() as block:
                @block.sync
                def _(sync):
                    for m in range(16):
                        if m >= 2:
                            sync.wait_ge(as_, as4 + m - 1)
                        sl = m % 2
                        sync.dma_start(out=resid[:, sl * 1024:(sl + 1) * 1024],
                                       in_=vsp[m]).then_inc(rsl[sl], 16)

                @block.scalar
                def _(scalar):
                    for m in range(16):
                        sl = m % 2
                        scalar.wait_ge(rsl[sl], 16 * (8 + m // 2 + 1))
                        if m >= 2:
                            scalar.wait_ge(ts, c_.val("ts") + m - 1)
                        scalar.activation(sqb[:, sl * 1024:(sl + 1) * 1024],
                                          resid[:, sl * 1024:(sl + 1) * 1024],
                                          AF.Square).then_inc(as_, 1)

                @block.tensor
                def _(tensor):
                    for m in range(16):
                        sl = m % 2
                        tensor.wait_ge(as_, as4 + m + 1)
                        for n in range(2):
                            ins = tensor.matmul(big1[0:1, n * 512:(n + 1) * 512], obf[:, :],
                                                sqb[:, sl * 1024 + n * 512:sl * 1024 + (n + 1) * 512],
                                                start=(m == 0), stop=(m == 15))
                        ins.then_inc(ts, 1)
            c_.inc("as_", 16)
            c_.inc("ts", 16)

            with nc.Block() as block:
                @block.vector
                def _(vector):
                    vector.tensor_scalar(row_sc[0:1, 1024:2048], row_sc[0:1, 1024:2048],
                                         scalar1=scl[0:1, 2:3], scalar2=None, op0=ALU.mult)
                    vector.tensor_scalar(row_sc[0:1, 2048:3072], big1[0:1, 0:1024],
                                         scalar1=scl[0:1, 2:3], scalar2=None, op0=ALU.mult)
            with nc.Block() as block:
                @block.vector
                def _(vector):
                    vector.tensor_tensor(row_sc[0:1, 0:1024], row_sc[0:1, 1024:2048],
                                         row_sc[0:1, 1024:2048], op=ALU.mult)
            with nc.Block() as block:
                @block.vector
                def _(vector):
                    vector.tensor_tensor(row_sc[0:1, 2048:3072], row_sc[0:1, 2048:3072],
                                         row_sc[0:1, 0:1024], op=ALU.subtract)
            with nc.Block() as block:
                @block.scalar
                def _(scalar):
                    scalar.activation(row_sc[0:1, 0:1024], row_sc[0:1, 2048:3072], AF.Sqrt,
                                      bias=scl[0:1, 1:2])
            with nc.Block() as block:
                @block.vector
                def _(vector):
                    vector.reciprocal(row_sc[0:1, 0:1024], row_sc[0:1, 0:1024])
            with nc.Block() as block:
                @block.vector
                def _(vector):
                    vector.tensor_tensor(row_sc[0:1, 3072:4096], row_sc[0:1, 1024:2048],
                                         row_sc[0:1, 0:1024], op=ALU.mult)
            with nc.Block() as block:
                @block.vector
                def _(vector):
                    vector.tensor_scalar(row_sc[0:1, 3072:4096], row_sc[0:1, 3072:4096],
                                         scalar1=-1.0, scalar2=None, op0=ALU.mult)
            with nc.Block() as block:
                @block.tensor
                def _(tensor):
                    for n in range(2):
                        tensor.matmul(big0[:, n * 512:(n + 1) * 512], onesf[:, :],
                                      row_sc[0:1, n * 512:(n + 1) * 512], start=True, stop=True)
                    for n in range(2):
                        ins = tensor.matmul(big0[:, 1024 + n * 512:1024 + (n + 1) * 512], onesf[:, :],
                                            row_sc[0:1, 3072 + n * 512:3072 + (n + 1) * 512],
                                            start=True, stop=True)
                    ins.then_inc(ts, 1)
                    c_.inc("ts")

                @block.vector
                def _(vector):
                    vector.wait_ge(ts, c_.val("ts"))
                    vector.tensor_copy(ab_b[:, 0:1024], big0[:, 0:1024])
                    vector.tensor_copy(ab_b[:, 1024:2048], big0[:, 1024:2048])

            vs4 = c_.val("vs")
            so4 = c_.val("so")
            with nc.Block() as block:
                @block.sync
                def _(sync):
                    for m in range(16):
                        if m >= 2:
                            sync.wait_ge(vs, vs4 + m - 1)
                        sl = m % 2
                        sync.dma_start(out=resid[:, sl * 1024:(sl + 1) * 1024],
                                       in_=vsp[m]).then_inc(rsl[sl], 16)

                @block.vector
                def _(vector):
                    for m in range(16):
                        sl = m % 2
                        vector.wait_ge(rsl[sl], 16 * (16 + m // 2 + 1))
                        if m >= 2:
                            vector.wait_ge(so, so4 + 16 * (m - 1))
                        vector.tensor_tensor(vbuf[:, sl * 1024:(sl + 1) * 1024],
                                             resid[:, sl * 1024:(sl + 1) * 1024],
                                             ab_b[:, 0:1024], op=ALU.mult)
                        vector.tensor_tensor(r1b[:, sl * 1024:(sl + 1) * 1024],
                                             vbuf[:, sl * 1024:(sl + 1) * 1024],
                                             ab_b[:, 1024:2048], op=ALU.add).then_inc(vs, 1)

                @block.scalar
                def _(scalar):
                    for m in range(16):
                        sl = m % 2
                        scalar.wait_ge(vs, vs4 + m + 1)
                        scalar.dma_start(out=r1_out[m], in_=r1b[:, sl * 1024:(sl + 1) * 1024]).then_inc(so, 16)
            c_.inc("vs", 16)
            c_.inc("so", 16 * 16)

            with nc.Block() as block:
                @block.sync
                def _(sync):
                    sync.wait_ge(so, c_.val("so"))
    return nc


def _prep_l2b(inputs, l1_results, l2a_results):
    W_out = np.asarray(inputs["L0_attn_W_out"])
    norm_w = np.asarray(inputs["L0_attn_norm_w"])
    Dp = np.asarray(inputs["L0_attn_Dp"])
    u = np.asarray(inputs["L0_attn_u"])

    woT = np.ascontiguousarray(W_out.T.astype(np.float32))
    wo = _retile_kt(woT)
    nw = np.ascontiguousarray(norm_w.reshape(32, 128).T.astype(np.float32))
    dprow = Dp.reshape(1, 64).astype(np.float32)
    id64f = np.eye(64, dtype=np.float32)
    idbf = np.eye(128, dtype=bf16)

    in_maps = []
    for c in range(NCORE):
        p, h = c // 2, c % 2
        uT = np.ascontiguousarray(u[p].T[:, h * TC:(h + 1) * TC])
        x_tiles = _feat_tiles(uT)
        if h == 0:
            carry = np.zeros((128, 4096), dtype=np.float32)
        else:
            carry = np.asarray(l2a_results[c - 1]["carry_out"]).reshape(128, 4096)
        r2a = l2a_results[c]
        in_maps.append({
            "zx": np.asarray(l1_results[c]["zx_spill"]).reshape(NCH, 67, 128, CHUNK),
            "m_in": np.asarray(r2a["m_spill"]).reshape(NCH, 2, 8, 128, 2048),
            "xcT_in": np.asarray(r2a["xcT_spill"]).reshape(NCH, 2, 128, 4096),
            "st_in": np.asarray(r2a["st_spill"]).reshape(NCH, 128, 4096),
            "dacs_in": np.asarray(r2a["dacs_out"]).reshape(NCH, 64, 256),
            "cfeat_in": np.asarray(r2a["cfeat_out"]).reshape(NCH, 128, 256),
            "carry_in": carry, "wo_in": wo, "nw_in": nw,
            "resid_in": x_tiles, "dprow_in": dprow,
            "id64f": id64f, "idbf": idbf, "idff": np.eye(128, dtype=np.float32),
        })
    return in_maps



def build_l3a():
    nc = _mk_nc()
    gp = nc.dram_tensor("gp", [1, 24], F32, kind="ExternalInput").ap()
    w1s = nc.dram_tensor("w1s", [2, 128, H0], F32, kind="ExternalInput").ap()
    w2s = nc.dram_tensor("w2s", [2, 128, H0], F32, kind="ExternalInput").ap()
    w3s = nc.dram_tensor("w3s", [6, 128, D], F32, kind="ExternalInput").ap()
    q1s = nc.dram_tensor("q1s", [2, 128, H0], BF16, kind="ExternalOutput").ap()
    q2s = nc.dram_tensor("q2s", [2, 128, H0], BF16, kind="ExternalOutput").ap()
    q3s = nc.dram_tensor("q3s", [6, 128, D], BF16, kind="ExternalOutput").ap()
    dbg = nc.dram_tensor("dbg", [2, 8], F32, kind="ExternalOutput").ap()

    c_ = CT()
    with ExitStack() as stk:
        sb = lambda name, shape, dt: stk.enter_context(nc.sbuf_tensor(name, shape, dt))
        sem = lambda name: stk.enter_context(nc.semaphore(name))
        gps = sb("gps", [1, 24], F32)
        throw_ = sb("throw_", [1, 8], F32)
        thc = sb("thc", [128, 8], F32)
        onesf = sb("onesf", [1, 128], F32)
        wb = sb("wb", [128, 2 * H0], F32)
        ge = sb("ge", [128, H0], F32)
        le = sb("le", [128, H0], F32)
        qb = sb("qb", [128, 2 * H0], BF16)
        big1 = stk.enter_context(nc.psum_tensor("big1", [128, 1024], F32))
        ld = sem("ld"); ts = sem("ts"); vs = sem("vs"); so = sem("so")

        with nc.Block() as block:
            @block.sync
            def _(sync):
                sync.dma_start(out=gps[:, :], in_=gp).then_inc(ld, 16)
                c_.inc("ld", 16)
                sync.wait_ge(ld, c_.val("ld"))

            @block.vector
            def _(vector):
                vector.memset(onesf[:, :], 1.0)

        with nc.Block() as block:
            @block.vector
            def _(vector):
                for j in range(3):
                    vector.tensor_reduce(throw_[0:1, 2 * j:2 * j + 1],
                                         gps[0:1, j * 8:(j + 1) * 8], axis=AX.X, op=ALU.add)
        with nc.Block() as block:
            @block.vector
            def _(vector):
                for j in range(3):
                    vector.tensor_scalar(throw_[0:1, 2 * j:2 * j + 1],
                                         throw_[0:1, 2 * j:2 * j + 1],
                                         scalar1=0.5 / (5632.0 * 2048.0), scalar2=0.5e-8,
                                         op0=ALU.mult, op1=ALU.add)
        with nc.Block() as block:
            @block.vector
            def _(vector):
                for j in range(3):
                    vector.tensor_scalar(throw_[0:1, 2 * j + 1:2 * j + 2],
                                         throw_[0:1, 2 * j:2 * j + 1],
                                         scalar1=-1.0, scalar2=None, op0=ALU.mult)

        with nc.Block() as block:
            @block.tensor
            def _(tensor):
                ins = tensor.matmul(big1[:, 0:6], onesf[:, :], throw_[0:1, 0:6], start=True, stop=True)
                ins.then_inc(ts, 1)
                c_.inc("ts")

            @block.vector
            def _(vector):
                vector.wait_ge(ts, c_.val("ts"))
                vector.tensor_copy(thc[:, 0:6], big1[:, 0:6])

        def quant(block, w_ap, q_ap, ntile, wid, F):
            @block.sync
            def _(sync):
                for t in range(ntile):
                    if t >= 2:
                        sync.wait_ge(vs, c_.val("vs") + t - 1)
                    sync.dma_start(out=wb[:, (t % 2) * F:(t % 2) * F + F],
                                   in_=w_ap[t]).then_inc(ld, 16)

            ld0 = c_.val("ld")
            c_.inc("ld", 16 * ntile)

            @block.vector
            def _(vector):
                for t in range(ntile):
                    vector.wait_ge(ld, ld0 + 16 * (t + 1))
                    sl = (t % 2) * F
                    vector.tensor_scalar(ge[:, 0:F], wb[:, sl:sl + F],
                                         scalar1=thc[:, 2 * wid:2 * wid + 1], scalar2=None, op0=ALU.is_ge)
                    vector.tensor_scalar(le[:, 0:F], wb[:, sl:sl + F],
                                         scalar1=thc[:, 2 * wid + 1:2 * wid + 2], scalar2=None, op0=ALU.is_le)
                    if t >= 2:
                        vector.wait_ge(so, c_.val("so") + 16 * (t - 1))
                    vector.tensor_tensor(qb[:, sl:sl + F], ge[:, 0:F], le[:, 0:F],
                                         op=ALU.subtract).then_inc(vs, 1)

            c_.inc("vs", ntile)

            @block.scalar
            def _(scalar):
                for t in range(ntile):
                    scalar.wait_ge(vs, c_.val("vs") - ntile + t + 1)
                    scalar.dma_start(out=q_ap[t], in_=qb[:, (t % 2) * F:(t % 2) * F + F]).then_inc(so, 16)

            c_.inc("so", 16 * ntile)

        with nc.Block() as block:
            @block.sync
            def _(sync):
                sync.dma_start(out=dbg[0:1], in_=throw_[0:1, :]).then_inc(so, 16)
                sync.dma_start(out=dbg[1:2], in_=thc[0:1, 0:8]).then_inc(so, 16)
                c_.inc("so", 32)

        with nc.Block() as block:
            quant(block, w1s, q1s, 2, 0, H0)
        with nc.Block() as block:
            quant(block, w2s, q2s, 2, 1, H0)
        with nc.Block() as block:
            quant(block, w3s, q3s, 6, 2, D)

        with nc.Block() as block:
            @block.sync
            def _(sync):
                sync.wait_ge(so, c_.val("so"))
    return nc


def _prep_l3a(inputs, l1_results):
    w1 = np.asarray(inputs["L0_ffn_w1"])
    w2 = np.asarray(inputs["L0_ffn_w2"])
    w3 = np.asarray(inputs["L0_ffn_w3"])
    w1T = w1.T; w2T = w2.T; w3T = w3.T
    gp = np.concatenate([np.concatenate([np.asarray(l1_results[c]["gpart"]).reshape(1, 8)[0, j:j + 1]
                                   for c in range(8)]) for j in range(3)]).reshape(1, 24).astype(np.float32)
    in_maps = []
    for c in range(NCORE):
        w1sl = np.ascontiguousarray(w1T[c * 256:(c + 1) * 256].reshape(2, 128, H0).astype(np.float32))
        w2sl = np.ascontiguousarray(w2T[c * 256:(c + 1) * 256].reshape(2, 128, H0).astype(np.float32))
        w3sl = np.zeros((6, 128, D), np.float32)
        w3sl.reshape(768, D)[0:704] = w3T[c * 704:(c + 1) * 704]
        in_maps.append({"gp": gp, "w1s": w1sl, "w2s": w2sl, "w3s": w3sl})
    return in_maps



def build_l3():
    nc = _mk_nc()
    r_in = nc.dram_tensor("r_in", [KT, 128, TC], BF16, kind="ExternalInput").ap()
    q1 = nc.dram_tensor("q1", [44, KT, 128, 128], BF16, kind="ExternalInput").ap()
    q2 = nc.dram_tensor("q2", [44, KT, 128, 128], BF16, kind="ExternalInput").ap()
    q3 = nc.dram_tensor("q3", [KT, 44, 128, 128], BF16, kind="ExternalInput").ap()
    r2_out = nc.dram_tensor("r2_out", [KT, 128, TC], BF16, kind="ExternalOutput").ap()

    c_ = CT()
    with ExitStack() as stk:
        sb = lambda name, shape, dt: stk.enter_context(nc.sbuf_tensor(name, shape, dt))
        sem = lambda name: stk.enter_context(nc.semaphore(name))
        xsb = sb("xsb", [128, KT * TC], BF16)
        hb = sb("hb", [128, 44 * TC], BF16)
        w12 = sb("w12", [128, 4 * KT * 128], BF16)
        w3b = sb("w3b", [128, 2 * 44 * 128], BF16)
        sg = sb("sg", [128, 2 * 512], F32)
        vbf = sb("vbf", [128, 2 * 1024], BF16)
        sqb = sb("sqb", [128, 2 * 1024], BF16)
        r2b = sb("r2b", [128, 2 * 1024], BF16)
        vv = sb("vv", [128, 2 * 1024], F32)
        onesf = sb("onesf", [1, 128], F32)
        obf = sb("obf", [128, 1], BF16)
        scl = sb("scl", [1, 4], F32)
        row_sc = sb("row_sc", [1, 4 * 1024], F32)
        ab_b = sb("ab_b", [128, 2 * 1024], F32)
        big0 = stk.enter_context(nc.psum_tensor("big0", [128, 2048], F32))
        big1 = stk.enter_context(nc.psum_tensor("big1", [128, 1024], F32))
        ld = sem("ld"); ldg = sem("ldg"); ts = sem("ts"); vs = sem("vs")
        as_ = sem("as_"); so = sem("so")
        wsl = [sem("wslA"), sem("wslB")]

        with nc.Block() as block:
            @block.sync
            def _(sync):
                sync.dma_start(out=xsb[:, :].rearrange("p (k t) -> p k t", k=KT),
                               in_=r_in.rearrange("k p t -> p k t")).then_inc(ld, 16)
                c_.inc("ld", 16)
                sync.wait_ge(ld, c_.val("ld"))

            @block.vector
            def _(vector):
                vector.memset(onesf[:, :], 1.0)
                vector.memset(obf[:, :], 1.0)
                vector.memset(scl[0:1, 0:1], 1.0 / 2048.0)
                vector.memset(scl[0:1, 1:2], 1e-5)

        ts0 = c_.val("ts"); vs0 = c_.val("vs"); as0 = c_.val("as_")
        with nc.Block() as block:
            @block.gpsimd
            def _(gpsimd):
                for mh in range(44):
                    if mh >= 2:
                        gpsimd.wait_ge(ts, ts0 + 2 * (mh - 1))
                    sl = mh % 2
                    gpsimd.dma_start(
                        out=w12[:, sl * 2 * KT * 128:sl * 2 * KT * 128 + KT * 128].rearrange(
                            "p (k mc) -> p k mc", k=KT),
                        in_=q1[mh].rearrange("k p mc -> p k mc")).then_inc(wsl[sl], 16)
                    gpsimd.dma_start(
                        out=w12[:, sl * 2 * KT * 128 + KT * 128:(sl + 1) * 2 * KT * 128].rearrange(
                            "p (k mc) -> p k mc", k=KT),
                        in_=q2[mh].rearrange("k p mc -> p k mc")).then_inc(wsl[sl], 16)

            @block.tensor
            def _(tensor):
                for mh in range(44):
                    sl = mh % 2
                    tensor.wait_ge(wsl[sl], 32 * (mh // 2 + 1))
                    for n in range(2):
                        u = 2 * mh + n
                        if u >= 2:
                            tensor.wait_ge(vs, vs0 + u - 1)
                        for k in range(KT):
                            tensor.matmul(big0[:, (u % 2) * 1024:(u % 2) * 1024 + 512],
                                          w12[:, sl * 2 * KT * 128 + k * 128:sl * 2 * KT * 128 + (k + 1) * 128],
                                          xsb[:, k * TC + n * 512:k * TC + (n + 1) * 512],
                                          start=(k == 0), stop=(k == KT - 1))
                        for k in range(KT):
                            ins = tensor.matmul(big0[:, (u % 2) * 1024 + 512:(u % 2) * 1024 + 1024],
                                                w12[:, sl * 2 * KT * 128 + (KT + k) * 128:sl * 2 * KT * 128 + (KT + k + 1) * 128],
                                                xsb[:, k * TC + n * 512:k * TC + (n + 1) * 512],
                                                start=(k == 0), stop=(k == KT - 1))
                        ins.then_inc(ts, 1)

            @block.scalar
            def _(scalar):
                for mh in range(44):
                    for n in range(2):
                        u = 2 * mh + n
                        scalar.wait_ge(ts, ts0 + u + 1)
                        if u >= 2:
                            scalar.wait_ge(vs, vs0 + u - 1)
                        scalar.activation(sg[:, (u % 2) * 512:(u % 2) * 512 + 512],
                                          big0[:, (u % 2) * 1024:(u % 2) * 1024 + 512],
                                          AF.Sigmoid).then_inc(as_, 1)

            @block.vector
            def _(vector):
                for mh in range(44):
                    for n in range(2):
                        u = 2 * mh + n
                        vector.wait_ge(as_, as0 + u + 1)
                        vector.tensor_tensor(hb[:, mh * TC + n * 512:mh * TC + (n + 1) * 512],
                                             sg[:, (u % 2) * 512:(u % 2) * 512 + 512],
                                             big0[:, (u % 2) * 1024 + 512:(u % 2) * 1024 + 1024],
                                             op=ALU.mult).then_inc(vs, 1)

        c_.inc("ts", 88); c_.inc("vs", 88); c_.inc("as_", 88)

        ts1 = c_.val("ts"); vs1 = c_.val("vs")
        with nc.Block() as block:
            @block.gpsimd
            def _(gpsimd):
                for m in range(16):
                    if m >= 2:
                        gpsimd.wait_ge(ts, ts1 + 2 * (m - 1))
                    sl = m % 2
                    gpsimd.dma_start(
                        out=w3b[:, sl * 44 * 128:(sl + 1) * 44 * 128].rearrange(
                            "p (k mc) -> p k mc", k=44),
                        in_=q3[m].rearrange("k p mc -> p k mc")).then_inc(wsl[sl], 16)

            @block.tensor
            def _(tensor):
                for m in range(16):
                    sl = m % 2
                    tensor.wait_ge(wsl[sl], 704 + 16 * (m // 2 + 1))
                    for n in range(2):
                        u = 2 * m + n
                        if u >= 4:
                            tensor.wait_ge(vs, vs1 + u - 3)
                        for k in range(44):
                            ins = tensor.matmul(big0[:, (u % 4) * 512:(u % 4 + 1) * 512],
                                                w3b[:, sl * 44 * 128 + k * 128:sl * 44 * 128 + (k + 1) * 128],
                                                hb[:, k * TC + n * 512:k * TC + (n + 1) * 512],
                                                start=(k == 0), stop=(k == 43))
                        ins.then_inc(ts, 1)

            @block.vector
            def _(vector):
                for m in range(16):
                    for n in range(2):
                        u = 2 * m + n
                        vector.wait_ge(ts, ts1 + u + 1)
                        vector.scalar_tensor_tensor(
                            xsb[:, m * TC + n * 512:m * TC + (n + 1) * 512],
                            big0[:, (u % 4) * 512:(u % 4 + 1) * 512], 1.0,
                            xsb[:, m * TC + n * 512:m * TC + (n + 1) * 512],
                            op0=ALU.mult, op1=ALU.add).then_inc(vs, 1)

        c_.inc("ts", 32); c_.inc("vs", 32)

        with nc.Block() as block:
            @block.tensor
            def _(tensor):
                for m in range(16):
                    for n in range(2):
                        ins = tensor.matmul(big1[0:1, n * 512:(n + 1) * 512], obf[:, :],
                                            xsb[:, m * TC + n * 512:m * TC + (n + 1) * 512],
                                            start=(m == 0), stop=(m == 15))
                ins.then_inc(ts, 1)
                c_.inc("ts")

            @block.vector
            def _(vector):
                vector.wait_ge(ts, c_.val("ts"))
                vector.tensor_copy(row_sc[0:1, 1024:2048], big1[0:1, 0:1024])

        as2 = c_.val("as_"); ts2 = c_.val("ts")
        with nc.Block() as block:
            @block.scalar
            def _(scalar):
                for m in range(16):
                    if m >= 2:
                        scalar.wait_ge(ts, ts2 + m - 1)
                    scalar.activation(sqb[:, (m % 2) * 1024:(m % 2 + 1) * 1024],
                                      xsb[:, m * TC:(m + 1) * TC],
                                      AF.Square).then_inc(as_, 1)

            @block.tensor
            def _(tensor):
                for m in range(16):
                    tensor.wait_ge(as_, as2 + m + 1)
                    for n in range(2):
                        ins = tensor.matmul(big1[0:1, n * 512:(n + 1) * 512], obf[:, :],
                                            sqb[:, (m % 2) * 1024 + n * 512:(m % 2) * 1024 + (n + 1) * 512],
                                            start=(m == 0), stop=(m == 15))
                    ins.then_inc(ts, 1)
        c_.inc("as_", 16); c_.inc("ts", 16)

        with nc.Block() as block:
            @block.vector
            def _(vector):
                vector.tensor_scalar(row_sc[0:1, 1024:2048], row_sc[0:1, 1024:2048],
                                     scalar1=scl[0:1, 0:1], scalar2=None, op0=ALU.mult)
                vector.tensor_scalar(row_sc[0:1, 2048:3072], big1[0:1, 0:1024],
                                     scalar1=scl[0:1, 0:1], scalar2=None, op0=ALU.mult)
        with nc.Block() as block:
            @block.vector
            def _(vector):
                vector.tensor_tensor(row_sc[0:1, 0:1024], row_sc[0:1, 1024:2048],
                                     row_sc[0:1, 1024:2048], op=ALU.mult)
        with nc.Block() as block:
            @block.vector
            def _(vector):
                vector.tensor_tensor(row_sc[0:1, 2048:3072], row_sc[0:1, 2048:3072],
                                     row_sc[0:1, 0:1024], op=ALU.subtract)
        with nc.Block() as block:
            @block.scalar
            def _(scalar):
                scalar.activation(row_sc[0:1, 0:1024], row_sc[0:1, 2048:3072], AF.Sqrt,
                                  bias=scl[0:1, 1:2])
        with nc.Block() as block:
            @block.vector
            def _(vector):
                vector.reciprocal(row_sc[0:1, 0:1024], row_sc[0:1, 0:1024])
        with nc.Block() as block:
            @block.vector
            def _(vector):
                vector.tensor_tensor(row_sc[0:1, 3072:4096], row_sc[0:1, 1024:2048],
                                     row_sc[0:1, 0:1024], op=ALU.mult)
        with nc.Block() as block:
            @block.vector
            def _(vector):
                vector.tensor_scalar(row_sc[0:1, 3072:4096], row_sc[0:1, 3072:4096],
                                     scalar1=-1.0, scalar2=None, op0=ALU.mult)
        with nc.Block() as block:
            @block.tensor
            def _(tensor):
                for n in range(2):
                    tensor.matmul(big0[:, n * 512:(n + 1) * 512], onesf[:, :],
                                  row_sc[0:1, n * 512:(n + 1) * 512], start=True, stop=True)
                for n in range(2):
                    ins = tensor.matmul(big0[:, 1024 + n * 512:1024 + (n + 1) * 512], onesf[:, :],
                                        row_sc[0:1, 3072 + n * 512:3072 + (n + 1) * 512],
                                        start=True, stop=True)
                ins.then_inc(ts, 1)
                c_.inc("ts")

            @block.vector
            def _(vector):
                vector.wait_ge(ts, c_.val("ts"))
                vector.tensor_copy(ab_b[:, 0:1024], big0[:, 0:1024])
                vector.tensor_copy(ab_b[:, 1024:2048], big0[:, 1024:2048])

        vs4 = c_.val("vs"); so4 = c_.val("so")
        with nc.Block() as block:
            @block.vector
            def _(vector):
                for m in range(16):
                    sl = m % 2
                    if m >= 2:
                        vector.wait_ge(so, so4 + 16 * (m - 1))
                    vector.tensor_tensor(vv[:, sl * 1024:(sl + 1) * 1024],
                                         xsb[:, m * TC:(m + 1) * TC],
                                         ab_b[:, 0:1024], op=ALU.mult)
                    vector.tensor_tensor(r2b[:, sl * 1024:(sl + 1) * 1024],
                                         vv[:, sl * 1024:(sl + 1) * 1024],
                                         ab_b[:, 1024:2048], op=ALU.add).then_inc(vs, 1)

            @block.scalar
            def _(scalar):
                for m in range(16):
                    sl = m % 2
                    scalar.wait_ge(vs, vs4 + m + 1)
                    scalar.dma_start(out=r2_out[m], in_=r2b[:, sl * 1024:(sl + 1) * 1024]).then_inc(so, 16)
        c_.inc("vs", 16); c_.inc("so", 256)

        with nc.Block() as block:
            @block.sync
            def _(sync):
                sync.wait_ge(so, c_.val("so"))
    return nc


def _prep_l3(inputs, l2b_results, l3a_results):
    q1T = np.concatenate([np.asarray(l3a_results[c]["q1s"]).reshape(256, H0) for c in range(8)])
    q2T = np.concatenate([np.asarray(l3a_results[c]["q2s"]).reshape(256, H0) for c in range(8)])
    q3T = np.concatenate([np.asarray(l3a_results[c]["q3s"]).reshape(768, D)[0:704] for c in range(8)])
    q1r = _retile_kt(q1T)
    q2r = _retile_kt(q2T)
    q3r = _retile_kt(q3T)
    in_maps = []
    for c in range(NCORE):
        in_maps.append({
            "r_in": np.asarray(l2b_results[c]["r1_out"]).reshape(KT, 128, TC),
            "q1": q1r, "q2": q2r, "q3": q3r,
        })
    return in_maps



def build_l4a():
    nc = _mk_nc()
    xf_in = nc.dram_tensor("xf_in", [KT, 128, S], BF16, kind="ExternalInput").ap()
    wq_in = nc.dram_tensor("wq_in", [8, KT, 128, 128], F32, kind="ExternalInput").ap()
    wk_in = nc.dram_tensor("wk_in", [8, KT, 128, 128], F32, kind="ExternalInput").ap()
    wv_in = nc.dram_tensor("wv_in", [KT, 128, 1024], F32, kind="ExternalInput").ap()
    bqk_in = nc.dram_tensor("bqk_in", [128, 16], F32, kind="ExternalInput").ap()
    bv_in = nc.dram_tensor("bv_in", [1, 1024], F32, kind="ExternalInput").ap()
    cos_in = nc.dram_tensor("cos_in", [128, S], BF16, kind="ExternalInput").ap()
    sin_in = nc.dram_tensor("sin_in", [128, S], BF16, kind="ExternalInput").ap()
    trimask = nc.dram_tensor("trimask", [128, 128], F32, kind="ExternalInput").ap()
    idbf = nc.dram_tensor("idbf", [128, 128], BF16, kind="ExternalInput").ap()
    ctx_out = nc.dram_tensor("ctx_out", [8, 128, S], BF16, kind="ExternalOutput").ap()

    c_ = CT()
    with ExitStack() as stk:
        sb = lambda name, shape, dt: stk.enter_context(nc.sbuf_tensor(name, shape, dt))
        sem = lambda name: stk.enter_context(nc.semaphore(name))
        q_feat = sb("q_feat", [128, 8 * S], BF16)
        k_feat = sb("k_feat", [128, 8 * S], BF16)
        v_tok = sb("v_tok", [128, 16 * 1024], BF16)
        idb = sb("idb", [128, 128], BF16)
        cosb = sb("cosb", [128, S], BF16)
        sinb = sb("sinb", [128, S], BF16)
        bqk = sb("bqk", [128, 16], F32)
        bvr = sb("bvr", [1, 1024], BF16)
        trim = sb("trim", [128, 128], F32)
        sc128 = sb("sc128", [128, 1], F32)
        onebf = sb("onebf", [1, 128], BF16)
        obf = sb("obf", [128, 1], BF16)
        onef1 = sb("onef1", [1, 1], F32)
        big0 = stk.enter_context(nc.psum_tensor("big0", [128, 2048], F32))
        big1 = stk.enter_context(nc.psum_tensor("big1", [128, 1024], F32))
        pbf = stk.enter_context(nc.psum_tensor("pbf", [128, 2048], BF16))
        ld = sem("ld"); ldg = sem("ldg"); ts = sem("ts"); vs = sem("vs")
        as_ = sem("as_"); so = sem("so")
        wsl = [sem("wslA"), sem("wslB")]

        with nc.Block() as block:
            @block.sync
            def _(sync):
                sync.dma_start(out=idb[:, :], in_=idbf).then_inc(ld, 16)
                sync.dma_start(out=cosb[:, :], in_=cos_in).then_inc(ld, 16)
                sync.dma_start(out=sinb[:, :], in_=sin_in).then_inc(ld, 16)
                sync.dma_start(out=bqk[:, :], in_=bqk_in).then_inc(ld, 16)
                sync.dma_start(out=trim[:, :], in_=trimask).then_inc(ld, 16)
                c_.inc("ld", 80)
                sync.wait_ge(ld, c_.val("ld"))

            @block.gpsimd
            def _(gpsimd):
                gpsimd.dma_start(out=bvr[:, :], in_=bv_in).then_inc(ldg, 16)
                c_.inc("ldg", 16)

            @block.vector
            def _(vector):
                vector.memset(sc128[:, :], 1.0 / (128.0 ** 0.5))
                vector.memset(onebf[:, :], 1.0)
                vector.memset(obf[:, :], 1.0)
                vector.memset(onef1[:, :], 1.0)

        with ExitStack() as stkA:
            wvb = stkA.enter_context(nc.sbuf_tensor("wvb", [128, KT * 1024], BF16))
            xtt = stkA.enter_context(nc.sbuf_tensor("xtt", [128, 2 * KT * 128], BF16))
            with nc.Block() as block:
                @block.gpsimd
                def _(gpsimd):
                    gpsimd.dma_start(out=wvb[:, :].rearrange("p (k c) -> p k c", k=KT),
                                     in_=wv_in.rearrange("k p c -> p k c")).then_inc(ldg, 16)
                    c_.inc("ldg", 16)

            ts0 = c_.val("ts"); vs0 = c_.val("vs"); ld0 = c_.val("ld")
            with nc.Block() as block:
                @block.sync
                def _(sync):
                    for tt in range(16):
                        if tt >= 2:
                            sync.wait_ge(ts, ts0 + 2 * (tt - 1))
                        sl = tt % 2
                        sync.dma_start(
                            out=xtt[:, sl * KT * 128:(sl + 1) * KT * 128].rearrange(
                                "p (k c) -> p k c", k=KT),
                            in_=bass.AP(xf_in.tensor, tt * 128,
                                        [[S, 128], [128 * S, KT], [1, 128]])).then_inc(ld, 16)
                c_.inc("ld", 256)

                @block.tensor
                def _(tensor):
                    tensor.wait_ge(ldg, c_.val("ldg"))
                    for tt in range(16):
                        sl = tt % 2
                        tensor.wait_ge(ld, ld0 + 16 * (tt + 1))
                        for n in range(2):
                            u = 2 * tt + n
                            if u >= 4:
                                tensor.wait_ge(vs, vs0 + u - 3)
                            for k in range(KT):
                                tensor.matmul(big0[:, (u % 4) * 512:(u % 4 + 1) * 512],
                                              xtt[:, sl * KT * 128 + k * 128:sl * KT * 128 + (k + 1) * 128],
                                              wvb[:, k * 1024 + n * 512:k * 1024 + (n + 1) * 512],
                                              start=(k == 0), stop=False)
                            ins = tensor.matmul(big0[:, (u % 4) * 512:(u % 4 + 1) * 512],
                                                onebf[:, :],
                                                bvr[0:1, n * 512:(n + 1) * 512],
                                                start=False, stop=True)
                            ins.then_inc(ts, 1)

                @block.vector
                def _(vector):
                    for tt in range(16):
                        for n in range(2):
                            u = 2 * tt + n
                            vector.wait_ge(ts, ts0 + u + 1)
                            vector.tensor_copy(v_tok[:, tt * 1024 + n * 512:tt * 1024 + (n + 1) * 512],
                                               big0[:, (u % 4) * 512:(u % 4 + 1) * 512]).then_inc(vs, 1)
            c_.inc("ts", 32)
            c_.inc("vs", 32)

        with ExitStack() as stk2:
            xf = stk2.enter_context(nc.sbuf_tensor("xf", [128, KT * S], BF16))
            wqk = stk2.enter_context(nc.sbuf_tensor("wqk", [128, 2 * KT * 128], BF16))

            with nc.Block() as block:
                @block.sync
                def _(sync):
                    sync.dma_start(out=xf[:, :].rearrange("p (k t) -> p k t", k=KT),
                                   in_=xf_in.rearrange("k p t -> p k t")).then_inc(ld, 16)
                    c_.inc("ld", 16)
                    sync.wait_ge(ld, c_.val("ld"))

            ts1 = c_.val("ts"); as1 = c_.val("as_")
            with nc.Block() as block:
                @block.gpsimd
                def _(gpsimd):
                    for m in range(16):
                        if m >= 2:
                            gpsimd.wait_ge(ts, ts1 + 4 * (m - 1))
                        sl = m % 2
                        w_ap = wq_in[m] if m < 8 else wk_in[m - 8]
                        gpsimd.dma_start(
                            out=wqk[:, sl * KT * 128:(sl + 1) * KT * 128].rearrange(
                                "p (k mc) -> p k mc", k=KT),
                            in_=w_ap.rearrange("k p mc -> p k mc")).then_inc(wsl[sl], 16)

                @block.tensor
                def _(tensor):
                    for m in range(16):
                        sl = m % 2
                        tensor.wait_ge(wsl[sl], 16 * (m // 2 + 1))
                        for n in range(4):
                            u = 4 * m + n
                            if u >= 4:
                                tensor.wait_ge(as_, as1 + u - 3)
                            for k in range(KT):
                                ins = tensor.matmul(
                                    big0[:, (u % 4) * 512:(u % 4 + 1) * 512],
                                    wqk[:, sl * KT * 128 + k * 128:sl * KT * 128 + (k + 1) * 128],
                                    xf[:, k * S + n * 512:k * S + (n + 1) * 512],
                                    start=(k == 0), stop=(k == KT - 1))
                            ins.then_inc(ts, 1)

                @block.vector
                def _(vector):
                    for m in range(16):
                        dst = q_feat if m < 8 else k_feat
                        mm = m % 8
                        for n in range(4):
                            u = 4 * m + n
                            vector.wait_ge(ts, ts1 + u + 1)
                            vector.tensor_scalar(dst[:, mm * S + n * 512:mm * S + (n + 1) * 512],
                                                 big0[:, (u % 4) * 512:(u % 4 + 1) * 512],
                                                 scalar1=bqk[:, m:m + 1], scalar2=None,
                                                 op0=ALU.add).then_inc(as_, 1)
            c_.inc("ts", 64)
            c_.inc("as_", 64)

        with ExitStack() as stk3:
            tmpr = stk3.enter_context(nc.sbuf_tensor("tmpr", [128, 4 * S], BF16))
            gs = stk3.enter_context(nc.semaphore("gs"))
            for fi, feat in enumerate((q_feat, k_feat)):
                for i in range(4):
                    a = feat[:, (2 * i) * S:(2 * i + 1) * S]
                    b = feat[:, (2 * i + 1) * S:(2 * i + 2) * S]
                    t1 = tmpr[:, 0:S]; t2 = tmpr[:, S:2 * S]
                    t3 = tmpr[:, 2 * S:3 * S]; t4 = tmpr[:, 3 * S:4 * S]
                    with nc.Block() as block:
                        @block.vector
                        def _(vector):
                            vector.tensor_tensor(t1, a, cosb[:, :], op=ALU.mult)
                            vector.tensor_tensor(t2, a, sinb[:, :], op=ALU.mult)
                            vector.wait_ge(gs, c_.val("gs") + 1)
                            vector.tensor_tensor(a, t1, t3, op=ALU.subtract)
                            vector.tensor_tensor(b, t2, t4, op=ALU.add)

                        @block.gpsimd
                        def _(gpsimd):
                            gpsimd.tensor_tensor(t3, b, sinb[:, :], op=ALU.mult)
                            gpsimd.tensor_tensor(t4, b, cosb[:, :], op=ALU.mult).then_inc(gs, 1)
                    c_.inc("gs")
                    with nc.Block() as block:
                        @block.vector
                        def _(vector):
                            vector.tensor_copy(t1[0:64, :], feat[64:128, (2 * i) * S:(2 * i + 1) * S])
                            vector.tensor_copy(feat[64:128, (2 * i) * S:(2 * i + 1) * S],
                                               feat[0:64, (2 * i + 1) * S:(2 * i + 2) * S])
                            vector.tensor_copy(feat[0:64, (2 * i + 1) * S:(2 * i + 2) * S], t1[0:64, :])

        with ExitStack() as stk4:
            sb4 = lambda name, shape, dt: stk4.enter_context(nc.sbuf_tensor(name, shape, dt))
            sqt = sb4("sqt", [128, S], BF16)
            ctx_tok = sb4("ctx_tok", [128, 16 * 1024], BF16)
            qn = sb4("qn", [1, 2 * S], F32)
            knm = sb4("knm", [1, 4], F32)
            negb = sb4("negb", [128, 8 * 16], F32)
            pbuf = sb4("pbuf", [128, 2048], BF16)
            ptb = sb4("ptb", [128, 2048], BF16)
            dcols = sb4("dcols", [128, 8], F32)
            dre = sb4("dre", [128, 2], F32)

            for h in range(8):
                with nc.Block() as block:
                    @block.scalar
                    def _(scalar):
                        scalar.activation(sqt[:, :], q_feat[:, h * S:(h + 1) * S], AF.Square).then_inc(as_, 1)
                        c_.inc("as_")

                    @block.tensor
                    def _(tensor):
                        tensor.wait_ge(as_, c_.val("as_"))
                        for n in range(4):
                            ins = tensor.matmul(big0[0:1, n * 512:(n + 1) * 512], obf[:, :],
                                                sqt[:, n * 512:(n + 1) * 512], start=True, stop=True)
                        ins.then_inc(ts, 1)
                        c_.inc("ts")

                with nc.Block() as block:
                    @block.scalar
                    def _(scalar):
                        scalar.activation(qn[0:1, 0:S], big0[0:1, 0:S], AF.Sqrt)
                        scalar.activation(sqt[:, :], k_feat[:, h * S:(h + 1) * S], AF.Square).then_inc(as_, 1)
                        c_.inc("as_")

                    @block.tensor
                    def _(tensor):
                        tensor.wait_ge(as_, c_.val("as_"))
                        for n in range(4):
                            ins = tensor.matmul(big0[0:1, n * 512:(n + 1) * 512], obf[:, :],
                                                sqt[:, n * 512:(n + 1) * 512], start=True, stop=True)
                        ins.then_inc(ts, 1)
                        c_.inc("ts")

                with nc.Block() as block:
                    @block.scalar
                    def _(scalar):
                        scalar.activation(qn[0:1, S:2 * S], big0[0:1, 0:S], AF.Sqrt)
                with nc.Block() as block:
                    @block.vector
                    def _(vector):
                        vector.tensor_reduce(knm[0:1, 0:1], qn[0:1, S:2 * S], axis=AX.X, op=ALU.max)
                with nc.Block() as block:
                    @block.vector
                    def _(vector):
                        vector.tensor_single_scalar(knm[0:1, 1:2], knm[0:1, 0:1],
                                                    -1.0 / (128.0 ** 0.5), op=ALU.mult)
                with nc.Block() as block:
                    @block.vector
                    def _(vector):
                        vector.tensor_scalar(qn[0:1, 0:S], qn[0:1, 0:S],
                                             scalar1=knm[0:1, 1:2], scalar2=None, op0=ALU.mult)
                with nc.Block() as block:
                    @block.tensor
                    def _(tensor):
                        for qt in range(16):
                            ins = tensor.transpose(big1[0:128, 512 + qt:512 + qt + 1],
                                                   qn[0:1, qt * 128:(qt + 1) * 128], onef1[0:1, 0:1])
                        ins.then_inc(ts, 1)
                        c_.inc("ts")

                    @block.vector
                    def _(vector):
                        vector.wait_ge(ts, c_.val("ts"))
                        vector.tensor_copy(negb[:, h * 16:(h + 1) * 16], big1[:, 512:528]).then_inc(vs, 1)
                        c_.inc("vs")

            gch = [0]
            for h in range(8):
                for qt in range(16):
                    ctx = (qt + 1) * 128
                    nchk = (ctx + 511) // 512
                    i_fl = h * 16 + qt
                    ts_f = c_.val("ts"); as_f = c_.val("as_"); vs_f = c_.val("vs")
                    with nc.Block() as block:
                        @block.tensor
                        def _(tensor):
                            for kc in range(nchk):
                                w = min(512, ctx - kc * 512)
                                sl = kc % 2
                                if kc >= 2:
                                    tensor.wait_ge(as_, as_f + kc - 1)
                                tensor.matmul(big0[:, sl * 512:sl * 512 + w],
                                              q_feat[:, h * S + qt * 128:h * S + (qt + 1) * 128],
                                              k_feat[:, h * S + kc * 512:h * S + kc * 512 + w],
                                              start=True, stop=True).then_inc(ts, 1)
                                c_.inc("ts")

                        tsc = c_.val("ts")

                        @block.vector
                        def _(vector):
                            kc = nchk - 1
                            w = min(512, ctx - kc * 512)
                            sl = kc % 2
                            vector.wait_ge(ts, tsc)
                            vector.tensor_tensor(big0[:, sl * 512 + w - 128:sl * 512 + w],
                                                 big0[:, sl * 512 + w - 128:sl * 512 + w],
                                                 trim[:, :], op=ALU.add).then_inc(vs, 1)
                            c_.inc("vs")

                        vsc = c_.val("vs")

                        @block.scalar
                        def _(scalar):
                            for kc in range(nchk):
                                w = min(512, ctx - kc * 512)
                                sl = kc % 2
                                scalar.wait_ge(ts, ts_f + kc + 1)
                                if kc == nchk - 1:
                                    scalar.wait_ge(vs, vsc)
                                scalar.activation(pbuf[:, kc * 512:kc * 512 + w],
                                                  big0[:, sl * 512:sl * 512 + w],
                                                  AF.Exp, scale=sc128[:, 0:1],
                                                  bias=negb[:, h * 16 + qt:h * 16 + qt + 1],
                                                  accum_out=dcols[:, kc:kc + 1]).then_inc(as_, 1)
                                c_.inc("as_")

                    with nc.Block() as block:
                        @block.vector
                        def _(vector):
                            vector.tensor_reduce(dre[:, 0:1], dcols[:, 0:nchk], axis=AX.X,
                                                 op=ALU.add).then_inc(vs, 1)
                            vector.wait_ge(vs, c_.val("vs") + 1)
                            vector.reciprocal(dre[:, 1:2], dre[:, 0:1])
                        c_.inc("vs")

                        nbat = (qt + 1 + 7) // 8
                        ts_t = c_.val("ts"); vs_t = c_.val("vs")

                        @block.tensor
                        def _(tensor):
                            for g in range(nbat):
                                if g >= 2:
                                    tensor.wait_ge(vs, vs_t + g - 1)
                                nk = min(8, qt + 1 - g * 8)
                                for k in range(nk):
                                    kt = g * 8 + k
                                    ins = tensor.transpose(
                                        pbf[:, (g % 2) * 1024 + k * 128:(g % 2) * 1024 + (k + 1) * 128],
                                        pbuf[:, kt * 128:(kt + 1) * 128], idb[:, :])
                                ins.then_inc(ts, 1)
                            tensor.wait_ge(vs, vs_t + nbat)
                            for kt in range(qt + 1):
                                ins = tensor.matmul(big1[:, (i_fl % 2) * 512:(i_fl % 2) * 512 + 128],
                                                    ptb[:, kt * 128:(kt + 1) * 128],
                                                    v_tok[:, kt * 1024 + h * 128:kt * 1024 + (h + 1) * 128],
                                                    start=(kt == 0), stop=(kt == qt))
                            ins.then_inc(ts, 1)

                        @block.vector
                        def _(vector):
                            for g in range(nbat):
                                vector.wait_ge(ts, ts_t + g + 1)
                                nk = min(8, qt + 1 - g * 8)
                                vector.tensor_copy(ptb[:, g * 1024:g * 1024 + nk * 128],
                                                   pbf[:, (g % 2) * 1024:(g % 2) * 1024 + nk * 128]).then_inc(vs, 1)
                            vector.wait_ge(ts, ts_t + nbat + 1)
                            vector.tensor_scalar(ctx_tok[:, qt * 1024 + h * 128:qt * 1024 + (h + 1) * 128],
                                                 big1[:, (i_fl % 2) * 512:(i_fl % 2) * 512 + 128],
                                                 scalar1=dre[:, 1:2], scalar2=None,
                                                 op0=ALU.mult).then_inc(vs, 1)

                        c_.inc("ts", nbat + 1)
                        c_.inc("vs", nbat + 1)

            ts_c = c_.val("ts"); vs_c = c_.val("vs")
            with nc.Block() as block:
                @block.tensor
                def _(tensor):
                    for h in range(8):
                        for g in range(2):
                            gi = h * 2 + g
                            if gi >= 2:
                                tensor.wait_ge(vs, vs_c + gi - 1)
                            for k in range(8):
                                tt = g * 8 + k
                                ins = tensor.transpose(
                                    pbf[:, (gi % 2) * 1024 + k * 128:(gi % 2) * 1024 + (k + 1) * 128],
                                    ctx_tok[:, tt * 1024 + h * 128:tt * 1024 + (h + 1) * 128],
                                    idb[:, :])
                            ins.then_inc(ts, 1)

                @block.vector
                def _(vector):
                    so_c = c_.val("so")
                    for h in range(8):
                        for g in range(2):
                            gi = h * 2 + g
                            vector.wait_ge(ts, ts_c + gi + 1)
                            if gi >= 2:
                                vector.wait_ge(so, so_c + 16 * (gi - 1))
                            vector.tensor_copy(ptb[:, (gi % 2) * 1024:(gi % 2) * 1024 + 1024],
                                               pbf[:, (gi % 2) * 1024:(gi % 2) * 1024 + 1024]).then_inc(vs, 1)

                @block.sync
                def _(sync):
                    for h in range(8):
                        for g in range(2):
                            gi = h * 2 + g
                            sync.wait_ge(vs, vs_c + gi + 1)
                            sync.dma_start(out=bass.AP(ctx_out.tensor, h * 128 * S + g * 1024,
                                                       [[S, 128], [1, 1024]]),
                                           in_=ptb[:, (gi % 2) * 1024:(gi % 2) * 1024 + 1024]).then_inc(so, 16)
            c_.inc("ts", 16)
            c_.inc("vs", 16)
            c_.inc("so", 256)

            with nc.Block() as block:
                @block.sync
                def _(sync):
                    sync.wait_ge(so, c_.val("so"))
    return nc


def _rope_tables():
    inv = 1.0 / (10000.0 ** (np.arange(0, 128, 2) / 128.0))
    t = np.arange(S, dtype=np.float64)
    fr = np.outer(t, inv)
    cos = np.tile(np.cos(fr).T, (2, 1)).astype(bf16)
    sin = np.tile(np.sin(fr).T, (2, 1)).astype(bf16)
    return cos, sin


def _qk_cols(j, hc):
    tiles = []
    for i in range(4):
        h0, h1 = hc * 8 + 2 * i, hc * 8 + 2 * i + 1
        tiles.append(np.concatenate([np.arange(64) + j * 2048 + h0 * 128,
                                     np.arange(64) + j * 2048 + h1 * 128]))
        tiles.append(np.concatenate([np.arange(64, 128) + j * 2048 + h0 * 128,
                                     np.arange(64, 128) + j * 2048 + h1 * 128]))
    return np.concatenate(tiles)


def _prep_l4a(inputs, l3_results):
    Wqkv = np.asarray(inputs["L1_attn_Wqkv"])
    bqkv = np.asarray(inputs["L1_attn_bqkv"])
    cos, sin = _rope_tables()
    qrow = np.arange(128)
    trim = np.where(qrow[None, :] <= qrow[:, None], 0.0, -1e30).astype(np.float32)
    idbf = np.eye(128, dtype=bf16)
    in_maps = []
    for c in range(NCORE):
        p, hc = c // 2, c % 2
        xf = np.concatenate([np.asarray(l3_results[2 * p]["r2_out"]).reshape(KT, 128, TC),
                             np.asarray(l3_results[2 * p + 1]["r2_out"]).reshape(KT, 128, TC)],
                            axis=2)
        qc = _qk_cols(0, hc); kc = _qk_cols(1, hc)
        wq = _retile_kt(np.ascontiguousarray(Wqkv[:, qc]).astype(np.float32))
        wk = _retile_kt(np.ascontiguousarray(Wqkv[:, kc]).astype(np.float32))
        wv = np.ascontiguousarray(
            Wqkv[:, 4096 + hc * 1024:4096 + (hc + 1) * 1024].reshape(16, 128, 1024).astype(np.float32))
        bqk = np.stack([bqkv[qc].reshape(8, 128), bqkv[kc].reshape(8, 128)]).reshape(16, 128).T
        bv = bqkv[4096 + hc * 1024:4096 + (hc + 1) * 1024].reshape(1, 1024)
        in_maps.append({
            "xf_in": xf, "wq_in": wq, "wk_in": wk, "wv_in": wv,
            "bqk_in": np.ascontiguousarray(bqk).astype(np.float32),
            "bv_in": bv.astype(np.float32),
            "cos_in": cos, "sin_in": sin, "trimask": trim, "idbf": idbf,
        })
    return in_maps



def build_l4b():
    nc = _mk_nc()
    ctxf = nc.dram_tensor("ctxf", [KT, 128, TC], BF16, kind="ExternalInput").ap()
    wo_in = nc.dram_tensor("wo_in", [KT, KT, 128, 128], F32, kind="ExternalInput").ap()
    bo_in = nc.dram_tensor("bo_in", [128, 16], F32, kind="ExternalInput").ap()
    r_in = nc.dram_tensor("r_in", [KT, 128, TC], BF16, kind="ExternalInput").ap()
    r3_out = nc.dram_tensor("r3_out", [KT, 128, TC], BF16, kind="ExternalOutput").ap()

    c_ = CT()
    with ExitStack() as stk:
        sb = lambda name, shape, dt: stk.enter_context(nc.sbuf_tensor(name, shape, dt))
        sem = lambda name: stk.enter_context(nc.semaphore(name))
        xsb = sb("xsb", [128, KT * TC], BF16)
        cf = sb("cf", [128, KT * TC], BF16)
        wob = sb("wob", [128, 2 * KT * 128], BF16)
        bo = sb("bo", [128, 16], F32)
        sqb = sb("sqb", [128, 2 * 1024], BF16)
        r2b = sb("r2b", [128, 2 * 1024], BF16)
        vv = sb("vv", [128, 2 * 1024], F32)
        onesf = sb("onesf", [1, 128], F32)
        obf = sb("obf", [128, 1], BF16)
        scl = sb("scl", [1, 4], F32)
        row_sc = sb("row_sc", [1, 4 * 1024], F32)
        ab_b = sb("ab_b", [128, 2 * 1024], F32)
        big0 = stk.enter_context(nc.psum_tensor("big0", [128, 2048], F32))
        big1 = stk.enter_context(nc.psum_tensor("big1", [128, 1024], F32))
        ld = sem("ld"); ldg = sem("ldg"); ts = sem("ts"); vs = sem("vs")
        as_ = sem("as_"); so = sem("so")
        wsl = [sem("wslA"), sem("wslB")]

        with nc.Block() as block:
            @block.sync
            def _(sync):
                sync.dma_start(out=xsb[:, :].rearrange("p (k t) -> p k t", k=KT),
                               in_=r_in.rearrange("k p t -> p k t")).then_inc(ld, 16)
                sync.dma_start(out=cf[:, :].rearrange("p (k t) -> p k t", k=KT),
                               in_=ctxf.rearrange("k p t -> p k t")).then_inc(ld, 16)
                sync.dma_start(out=bo[:, :], in_=bo_in).then_inc(ld, 16)
                c_.inc("ld", 48)
                sync.wait_ge(ld, c_.val("ld"))

            @block.vector
            def _(vector):
                vector.memset(onesf[:, :], 1.0)
                vector.memset(obf[:, :], 1.0)
                vector.memset(scl[0:1, 0:1], 1.0 / 2048.0)
                vector.memset(scl[0:1, 1:2], 1e-5)

        ts1 = c_.val("ts"); vs1 = c_.val("vs"); as1 = c_.val("as_")
        with nc.Block() as block:
            @block.gpsimd
            def _(gpsimd):
                for m in range(16):
                    if m >= 2:
                        gpsimd.wait_ge(ts, ts1 + 2 * (m - 1))
                    sl = m % 2
                    gpsimd.dma_start(
                        out=wob[:, sl * KT * 128:(sl + 1) * KT * 128].rearrange(
                            "p (k mc) -> p k mc", k=KT),
                        in_=wo_in[m].rearrange("k p mc -> p k mc")).then_inc(wsl[sl], 16)

            @block.tensor
            def _(tensor):
                for m in range(16):
                    sl = m % 2
                    tensor.wait_ge(wsl[sl], 16 * (m // 2 + 1))
                    for n in range(2):
                        u = 2 * m + n
                        if u >= 4:
                            tensor.wait_ge(vs, vs1 + u - 3)
                        for k in range(KT):
                            ins = tensor.matmul(big0[:, (u % 4) * 512:(u % 4 + 1) * 512],
                                                wob[:, sl * KT * 128 + k * 128:sl * KT * 128 + (k + 1) * 128],
                                                cf[:, k * TC + n * 512:k * TC + (n + 1) * 512],
                                                start=(k == 0), stop=(k == KT - 1))
                        ins.then_inc(ts, 1)

            @block.vector
            def _(vector):
                for m in range(16):
                    for n in range(2):
                        u = 2 * m + n
                        vector.wait_ge(ts, ts1 + u + 1)
                        vector.scalar_tensor_tensor(
                            xsb[:, m * TC + n * 512:m * TC + (n + 1) * 512],
                            big0[:, (u % 4) * 512:(u % 4 + 1) * 512], bo[:, m:m + 1],
                            xsb[:, m * TC + n * 512:m * TC + (n + 1) * 512],
                            op0=ALU.add, op1=ALU.add).then_inc(vs, 1)
        c_.inc("ts", 32); c_.inc("vs", 32)
        with nc.Block() as block:
            @block.tensor
            def _(tensor):
                for m in range(16):
                    for n in range(2):
                        ins = tensor.matmul(big1[0:1, n * 512:(n + 1) * 512], obf[:, :],
                                            xsb[:, m * TC + n * 512:m * TC + (n + 1) * 512],
                                            start=(m == 0), stop=(m == 15))
                ins.then_inc(ts, 1)
                c_.inc("ts")

            @block.vector
            def _(vector):
                vector.wait_ge(ts, c_.val("ts"))
                vector.tensor_copy(row_sc[0:1, 1024:2048], big1[0:1, 0:1024])

        as2 = c_.val("as_"); ts2 = c_.val("ts")
        with nc.Block() as block:
            @block.scalar
            def _(scalar):
                for m in range(16):
                    if m >= 2:
                        scalar.wait_ge(ts, ts2 + m - 1)
                    scalar.activation(sqb[:, (m % 2) * 1024:(m % 2 + 1) * 1024],
                                      xsb[:, m * TC:(m + 1) * TC],
                                      AF.Square).then_inc(as_, 1)

            @block.tensor
            def _(tensor):
                for m in range(16):
                    tensor.wait_ge(as_, as2 + m + 1)
                    for n in range(2):
                        ins = tensor.matmul(big1[0:1, n * 512:(n + 1) * 512], obf[:, :],
                                            sqb[:, (m % 2) * 1024 + n * 512:(m % 2) * 1024 + (n + 1) * 512],
                                            start=(m == 0), stop=(m == 15))
                    ins.then_inc(ts, 1)
        c_.inc("as_", 16); c_.inc("ts", 16)

        with nc.Block() as block:
            @block.vector
            def _(vector):
                vector.tensor_scalar(row_sc[0:1, 1024:2048], row_sc[0:1, 1024:2048],
                                     scalar1=scl[0:1, 0:1], scalar2=None, op0=ALU.mult)
                vector.tensor_scalar(row_sc[0:1, 2048:3072], big1[0:1, 0:1024],
                                     scalar1=scl[0:1, 0:1], scalar2=None, op0=ALU.mult)
        with nc.Block() as block:
            @block.vector
            def _(vector):
                vector.tensor_tensor(row_sc[0:1, 0:1024], row_sc[0:1, 1024:2048],
                                     row_sc[0:1, 1024:2048], op=ALU.mult)
        with nc.Block() as block:
            @block.vector
            def _(vector):
                vector.tensor_tensor(row_sc[0:1, 2048:3072], row_sc[0:1, 2048:3072],
                                     row_sc[0:1, 0:1024], op=ALU.subtract)
        with nc.Block() as block:
            @block.scalar
            def _(scalar):
                scalar.activation(row_sc[0:1, 0:1024], row_sc[0:1, 2048:3072], AF.Sqrt,
                                  bias=scl[0:1, 1:2])
        with nc.Block() as block:
            @block.vector
            def _(vector):
                vector.reciprocal(row_sc[0:1, 0:1024], row_sc[0:1, 0:1024])
        with nc.Block() as block:
            @block.vector
            def _(vector):
                vector.tensor_tensor(row_sc[0:1, 3072:4096], row_sc[0:1, 1024:2048],
                                     row_sc[0:1, 0:1024], op=ALU.mult)
        with nc.Block() as block:
            @block.vector
            def _(vector):
                vector.tensor_scalar(row_sc[0:1, 3072:4096], row_sc[0:1, 3072:4096],
                                     scalar1=-1.0, scalar2=None, op0=ALU.mult)
        with nc.Block() as block:
            @block.tensor
            def _(tensor):
                for n in range(2):
                    tensor.matmul(big0[:, n * 512:(n + 1) * 512], onesf[:, :],
                                  row_sc[0:1, n * 512:(n + 1) * 512], start=True, stop=True)
                for n in range(2):
                    ins = tensor.matmul(big0[:, 1024 + n * 512:1024 + (n + 1) * 512], onesf[:, :],
                                        row_sc[0:1, 3072 + n * 512:3072 + (n + 1) * 512],
                                        start=True, stop=True)
                ins.then_inc(ts, 1)
                c_.inc("ts")

            @block.vector
            def _(vector):
                vector.wait_ge(ts, c_.val("ts"))
                vector.tensor_copy(ab_b[:, 0:1024], big0[:, 0:1024])
                vector.tensor_copy(ab_b[:, 1024:2048], big0[:, 1024:2048])

        vs4 = c_.val("vs"); so4 = c_.val("so")
        with nc.Block() as block:
            @block.vector
            def _(vector):
                for m in range(16):
                    sl = m % 2
                    if m >= 2:
                        vector.wait_ge(so, so4 + 16 * (m - 1))
                    vector.tensor_tensor(vv[:, sl * 1024:(sl + 1) * 1024],
                                         xsb[:, m * TC:(m + 1) * TC],
                                         ab_b[:, 0:1024], op=ALU.mult)
                    vector.tensor_tensor(r2b[:, sl * 1024:(sl + 1) * 1024],
                                         vv[:, sl * 1024:(sl + 1) * 1024],
                                         ab_b[:, 1024:2048], op=ALU.add).then_inc(vs, 1)

            @block.scalar
            def _(scalar):
                for m in range(16):
                    sl = m % 2
                    scalar.wait_ge(vs, vs4 + m + 1)
                    scalar.dma_start(out=r3_out[m], in_=r2b[:, sl * 1024:(sl + 1) * 1024]).then_inc(so, 16)
        c_.inc("vs", 16); c_.inc("so", 256)

        with nc.Block() as block:
            @block.sync
            def _(sync):
                sync.wait_ge(so, c_.val("so"))
    return nc


def _prep_l4b(inputs, l3_results, l4a_results):
    Wout = np.asarray(inputs["L1_attn_Wout"])
    bout = np.asarray(inputs["L1_attn_bout"])
    wo = _retile_kt(np.ascontiguousarray(Wout.astype(np.float32)))
    bo = np.ascontiguousarray(bout.reshape(16, 128).T.astype(np.float32))
    in_maps = []
    for c in range(NCORE):
        p, hc = c // 2, c % 2
        lo = np.asarray(l4a_results[2 * p]["ctx_out"]).reshape(8, 128, S)[:, :, hc * TC:(hc + 1) * TC]
        hi = np.asarray(l4a_results[2 * p + 1]["ctx_out"]).reshape(8, 128, S)[:, :, hc * TC:(hc + 1) * TC]
        ctx = np.ascontiguousarray(np.concatenate([lo, hi], axis=0))
        in_maps.append({
            "ctxf": ctx, "wo_in": wo, "bo_in": bo,
            "r_in": np.asarray(l3_results[c]["r2_out"]).reshape(KT, 128, TC),
        })
    return in_maps



def build_l5():
    nc = _mk_nc()
    r_in = nc.dram_tensor("r_in", [KT, 128, TC], BF16, kind="ExternalInput").ap()
    gw = nc.dram_tensor("gw", [44, KT, 128, 128], F32, kind="ExternalInput").ap()
    uw = nc.dram_tensor("uw", [44, KT, 128, 128], F32, kind="ExternalInput").ap()
    dw = nc.dram_tensor("dw", [KT, 44, 128, 128], F32, kind="ExternalInput").ap()
    gsc = nc.dram_tensor("gsc", [128, 44], F32, kind="ExternalInput").ap()
    usc = nc.dram_tensor("usc", [128, 44], F32, kind="ExternalInput").ap()
    dsc = nc.dram_tensor("dsc", [128, 16], F32, kind="ExternalInput").ap()
    r4_out = nc.dram_tensor("r4_out", [KT, 128, TC], F32, kind="ExternalOutput").ap()

    c_ = CT()
    with ExitStack() as stk:
        sb = lambda name, shape, dt: stk.enter_context(nc.sbuf_tensor(name, shape, dt))
        sem = lambda name: stk.enter_context(nc.semaphore(name))
        xsb = sb("xsb", [128, KT * TC], BF16)
        hb = sb("hb", [128, 44 * TC], BF16)
        w12 = sb("w12", [128, 4 * KT * 128], BF16)
        w3b = sb("w3b", [128, 2 * 44 * 128], BF16)
        gsb = sb("gsb", [128, 44], F32)
        usb = sb("usb", [128, 44], F32)
        dsb = sb("dsb", [128, 16], F32)
        sg = sb("sg", [128, 2 * 512], F32)
        sqb = sb("sqb", [128, 2 * 1024], BF16)
        r4b = sb("r4b", [128, 2 * 1024], F32)
        vv = sb("vv", [128, 2 * 1024], F32)
        onesf = sb("onesf", [1, 128], F32)
        obf = sb("obf", [128, 1], BF16)
        scl = sb("scl", [1, 4], F32)
        row_sc = sb("row_sc", [1, 4 * 1024], F32)
        ab_b = sb("ab_b", [128, 2 * 1024], F32)
        big0 = stk.enter_context(nc.psum_tensor("big0", [128, 2048], F32))
        big1 = stk.enter_context(nc.psum_tensor("big1", [128, 1024], F32))
        ld = sem("ld"); ldg = sem("ldg"); ts = sem("ts"); vs = sem("vs")
        as_ = sem("as_"); so = sem("so")
        wsl = [sem("wslA"), sem("wslB")]

        with nc.Block() as block:
            @block.sync
            def _(sync):
                sync.dma_start(out=xsb[:, :].rearrange("p (k t) -> p k t", k=KT),
                               in_=r_in.rearrange("k p t -> p k t")).then_inc(ld, 16)
                sync.dma_start(out=gsb[:, :], in_=gsc).then_inc(ld, 16)
                sync.dma_start(out=usb[:, :], in_=usc).then_inc(ld, 16)
                sync.dma_start(out=dsb[:, :], in_=dsc).then_inc(ld, 16)
                c_.inc("ld", 64)
                sync.wait_ge(ld, c_.val("ld"))

            @block.vector
            def _(vector):
                vector.memset(onesf[:, :], 1.0)
                vector.memset(obf[:, :], 1.0)
                vector.memset(scl[0:1, 0:1], 1.0 / 2048.0)
                vector.memset(scl[0:1, 1:2], 1e-5)

        ts0 = c_.val("ts"); vs0 = c_.val("vs"); as0 = c_.val("as_")
        with nc.Block() as block:
            @block.gpsimd
            def _(gpsimd):
                for mh in range(44):
                    if mh >= 2:
                        gpsimd.wait_ge(ts, ts0 + 2 * (mh - 1))
                    sl = mh % 2
                    gpsimd.dma_start(
                        out=w12[:, sl * 2 * KT * 128:sl * 2 * KT * 128 + KT * 128].rearrange(
                            "p (k mc) -> p k mc", k=KT),
                        in_=gw[mh].rearrange("k p mc -> p k mc")).then_inc(wsl[sl], 16)
                    gpsimd.dma_start(
                        out=w12[:, sl * 2 * KT * 128 + KT * 128:(sl + 1) * 2 * KT * 128].rearrange(
                            "p (k mc) -> p k mc", k=KT),
                        in_=uw[mh].rearrange("k p mc -> p k mc")).then_inc(wsl[sl], 16)

            @block.tensor
            def _(tensor):
                for mh in range(44):
                    sl = mh % 2
                    tensor.wait_ge(wsl[sl], 32 * (mh // 2 + 1))
                    for n in range(2):
                        u = 2 * mh + n
                        if u >= 2:
                            tensor.wait_ge(vs, vs0 + u - 1)
                        for k in range(KT):
                            tensor.matmul(big0[:, (u % 2) * 1024:(u % 2) * 1024 + 512],
                                          w12[:, sl * 2 * KT * 128 + k * 128:sl * 2 * KT * 128 + (k + 1) * 128],
                                          xsb[:, k * TC + n * 512:k * TC + (n + 1) * 512],
                                          start=(k == 0), stop=(k == KT - 1))
                        for k in range(KT):
                            ins = tensor.matmul(big0[:, (u % 2) * 1024 + 512:(u % 2) * 1024 + 1024],
                                                w12[:, sl * 2 * KT * 128 + (KT + k) * 128:sl * 2 * KT * 128 + (KT + k + 1) * 128],
                                                xsb[:, k * TC + n * 512:k * TC + (n + 1) * 512],
                                                start=(k == 0), stop=(k == KT - 1))
                        ins.then_inc(ts, 1)

            @block.scalar
            def _(scalar):
                for mh in range(44):
                    for n in range(2):
                        u = 2 * mh + n
                        scalar.wait_ge(ts, ts0 + u + 1)
                        if u >= 2:
                            scalar.wait_ge(vs, vs0 + u - 1)
                        scalar.activation(sg[:, (u % 2) * 512:(u % 2) * 512 + 512],
                                          big0[:, (u % 2) * 1024:(u % 2) * 1024 + 512],
                                          AF.Silu, scale=gsb[:, mh:mh + 1]).then_inc(as_, 1)

            @block.vector
            def _(vector):
                for mh in range(44):
                    for n in range(2):
                        u = 2 * mh + n
                        vector.wait_ge(as_, as0 + u + 1)
                        vector.scalar_tensor_tensor(
                            hb[:, mh * TC + n * 512:mh * TC + (n + 1) * 512],
                            big0[:, (u % 2) * 1024 + 512:(u % 2) * 1024 + 1024],
                            usb[:, mh:mh + 1],
                            sg[:, (u % 2) * 512:(u % 2) * 512 + 512],
                            op0=ALU.mult, op1=ALU.mult).then_inc(vs, 1)

        c_.inc("ts", 88); c_.inc("vs", 88); c_.inc("as_", 88)

        ts1 = c_.val("ts"); vs1 = c_.val("vs")
        with nc.Block() as block:
            @block.gpsimd
            def _(gpsimd):
                for m in range(16):
                    if m >= 2:
                        gpsimd.wait_ge(ts, ts1 + 2 * (m - 1))
                    sl = m % 2
                    gpsimd.dma_start(
                        out=w3b[:, sl * 44 * 128:(sl + 1) * 44 * 128].rearrange(
                            "p (k mc) -> p k mc", k=44),
                        in_=dw[m].rearrange("k p mc -> p k mc")).then_inc(wsl[sl], 16)

            @block.tensor
            def _(tensor):
                for m in range(16):
                    sl = m % 2
                    tensor.wait_ge(wsl[sl], 704 + 16 * (m // 2 + 1))
                    for n in range(2):
                        u = 2 * m + n
                        if u >= 4:
                            tensor.wait_ge(vs, vs1 + u - 3)
                        for k in range(44):
                            ins = tensor.matmul(big0[:, (u % 4) * 512:(u % 4 + 1) * 512],
                                                w3b[:, sl * 44 * 128 + k * 128:sl * 44 * 128 + (k + 1) * 128],
                                                hb[:, k * TC + n * 512:k * TC + (n + 1) * 512],
                                                start=(k == 0), stop=(k == 43))
                        ins.then_inc(ts, 1)

            @block.vector
            def _(vector):
                for m in range(16):
                    for n in range(2):
                        u = 2 * m + n
                        vector.wait_ge(ts, ts1 + u + 1)
                        vector.scalar_tensor_tensor(
                            xsb[:, m * TC + n * 512:m * TC + (n + 1) * 512],
                            big0[:, (u % 4) * 512:(u % 4 + 1) * 512], dsb[:, m:m + 1],
                            xsb[:, m * TC + n * 512:m * TC + (n + 1) * 512],
                            op0=ALU.mult, op1=ALU.add).then_inc(vs, 1)

        c_.inc("ts", 32); c_.inc("vs", 32)
        with nc.Block() as block:
            @block.tensor
            def _(tensor):
                for m in range(16):
                    for n in range(2):
                        ins = tensor.matmul(big1[0:1, n * 512:(n + 1) * 512], obf[:, :],
                                            xsb[:, m * TC + n * 512:m * TC + (n + 1) * 512],
                                            start=(m == 0), stop=(m == 15))
                ins.then_inc(ts, 1)
                c_.inc("ts")

            @block.vector
            def _(vector):
                vector.wait_ge(ts, c_.val("ts"))
                vector.tensor_copy(row_sc[0:1, 1024:2048], big1[0:1, 0:1024])

        as2 = c_.val("as_"); ts2 = c_.val("ts")
        with nc.Block() as block:
            @block.scalar
            def _(scalar):
                for m in range(16):
                    if m >= 2:
                        scalar.wait_ge(ts, ts2 + m - 1)
                    scalar.activation(sqb[:, (m % 2) * 1024:(m % 2 + 1) * 1024],
                                      xsb[:, m * TC:(m + 1) * TC],
                                      AF.Square).then_inc(as_, 1)

            @block.tensor
            def _(tensor):
                for m in range(16):
                    tensor.wait_ge(as_, as2 + m + 1)
                    for n in range(2):
                        ins = tensor.matmul(big1[0:1, n * 512:(n + 1) * 512], obf[:, :],
                                            sqb[:, (m % 2) * 1024 + n * 512:(m % 2) * 1024 + (n + 1) * 512],
                                            start=(m == 0), stop=(m == 15))
                    ins.then_inc(ts, 1)
        c_.inc("as_", 16); c_.inc("ts", 16)

        with nc.Block() as block:
            @block.vector
            def _(vector):
                vector.tensor_scalar(row_sc[0:1, 1024:2048], row_sc[0:1, 1024:2048],
                                     scalar1=scl[0:1, 0:1], scalar2=None, op0=ALU.mult)
                vector.tensor_scalar(row_sc[0:1, 2048:3072], big1[0:1, 0:1024],
                                     scalar1=scl[0:1, 0:1], scalar2=None, op0=ALU.mult)
        with nc.Block() as block:
            @block.vector
            def _(vector):
                vector.tensor_tensor(row_sc[0:1, 0:1024], row_sc[0:1, 1024:2048],
                                     row_sc[0:1, 1024:2048], op=ALU.mult)
        with nc.Block() as block:
            @block.vector
            def _(vector):
                vector.tensor_tensor(row_sc[0:1, 2048:3072], row_sc[0:1, 2048:3072],
                                     row_sc[0:1, 0:1024], op=ALU.subtract)
        with nc.Block() as block:
            @block.scalar
            def _(scalar):
                scalar.activation(row_sc[0:1, 0:1024], row_sc[0:1, 2048:3072], AF.Sqrt,
                                  bias=scl[0:1, 1:2])
        with nc.Block() as block:
            @block.vector
            def _(vector):
                vector.reciprocal(row_sc[0:1, 0:1024], row_sc[0:1, 0:1024])
        with nc.Block() as block:
            @block.vector
            def _(vector):
                vector.tensor_tensor(row_sc[0:1, 3072:4096], row_sc[0:1, 1024:2048],
                                     row_sc[0:1, 0:1024], op=ALU.mult)
        with nc.Block() as block:
            @block.vector
            def _(vector):
                vector.tensor_scalar(row_sc[0:1, 3072:4096], row_sc[0:1, 3072:4096],
                                     scalar1=-1.0, scalar2=None, op0=ALU.mult)
        with nc.Block() as block:
            @block.tensor
            def _(tensor):
                for n in range(2):
                    tensor.matmul(big0[:, n * 512:(n + 1) * 512], onesf[:, :],
                                  row_sc[0:1, n * 512:(n + 1) * 512], start=True, stop=True)
                for n in range(2):
                    ins = tensor.matmul(big0[:, 1024 + n * 512:1024 + (n + 1) * 512], onesf[:, :],
                                        row_sc[0:1, 3072 + n * 512:3072 + (n + 1) * 512],
                                        start=True, stop=True)
                ins.then_inc(ts, 1)
                c_.inc("ts")

            @block.vector
            def _(vector):
                vector.wait_ge(ts, c_.val("ts"))
                vector.tensor_copy(ab_b[:, 0:1024], big0[:, 0:1024])
                vector.tensor_copy(ab_b[:, 1024:2048], big0[:, 1024:2048])

        vs4 = c_.val("vs"); so4 = c_.val("so")
        with nc.Block() as block:
            @block.vector
            def _(vector):
                for m in range(16):
                    sl = m % 2
                    if m >= 2:
                        vector.wait_ge(so, so4 + 16 * (m - 1))
                    vector.tensor_tensor(vv[:, sl * 1024:(sl + 1) * 1024],
                                         xsb[:, m * TC:(m + 1) * TC],
                                         ab_b[:, 0:1024], op=ALU.mult)
                    vector.tensor_tensor(r4b[:, sl * 1024:(sl + 1) * 1024],
                                         vv[:, sl * 1024:(sl + 1) * 1024],
                                         ab_b[:, 1024:2048], op=ALU.add).then_inc(vs, 1)

            @block.scalar
            def _(scalar):
                for m in range(16):
                    sl = m % 2
                    scalar.wait_ge(vs, vs4 + m + 1)
                    scalar.dma_start(out=r4_out[m], in_=r4b[:, sl * 1024:(sl + 1) * 1024]).then_inc(so, 16)
        c_.inc("vs", 16); c_.inc("so", 256)

        with nc.Block() as block:
            @block.sync
            def _(sync):
                sync.wait_ge(so, c_.val("so"))
    return nc


def _prep_l5(inputs, l4b_results):
    gwT = np.ascontiguousarray(np.asarray(inputs["L1_ffn_gate_w"]).T.astype(np.float32))
    uwT = np.ascontiguousarray(np.asarray(inputs["L1_ffn_up_w"]).T.astype(np.float32))
    dwT = np.ascontiguousarray(np.asarray(inputs["L1_ffn_down_w"]).T.astype(np.float32))
    gwr = _retile_kt(gwT); uwr = _retile_kt(uwT); dwr = _retile_kt(dwT)
    gsc = np.ascontiguousarray(np.asarray(inputs["L1_ffn_gate_s"]).reshape(44, 128).T.astype(np.float32))
    usc = np.ascontiguousarray(np.asarray(inputs["L1_ffn_up_s"]).reshape(44, 128).T.astype(np.float32))
    dsc = np.ascontiguousarray(np.asarray(inputs["L1_ffn_down_s"]).reshape(16, 128).T.astype(np.float32))
    in_maps = []
    for c in range(NCORE):
        in_maps.append({
            "r_in": np.asarray(l4b_results[c]["r3_out"]).reshape(KT, 128, TC),
            "gw": gwr, "uw": uwr, "dw": dwr, "gsc": gsc, "usc": usc, "dsc": dsc,
        })
    return in_maps


def kernel(**inputs):
    core_ids = list(range(NCORE))

    def run(nc, in_maps):
        res = run_bass_kernel_spmd(nc, in_maps, core_ids=core_ids)
        return res.results

    r1 = run(build_l1(), _prep_l1(inputs))
    r2a = run(build_l2a(), _prep_l2a(inputs, r1))
    r2b = run(build_l2b(), _prep_l2b(inputs, r1, r2a))
    r3a = run(build_l3a(), _prep_l3a(inputs, r1))
    r3 = run(build_l3(), _prep_l3(inputs, r2b, r3a))
    r4a = run(build_l4a(), _prep_l4a(inputs, r3))
    r4b = run(build_l4b(), _prep_l4b(inputs, r3, r4a))
    r5 = run(build_l5(), _prep_l5(inputs, r4b))

    out = np.empty((BATCH, S, D), dtype=np.float32)
    for c in range(NCORE):
        p, h = c // 2, c % 2
        out[p, h * TC:(h + 1) * TC, :] = \
            np.asarray(r5[c]["r4_out"]).reshape(D, TC).T
    return out
